# version 48
# baseline (speedup 1.0000x reference)
import numpy as np
import ml_dtypes
from contextlib import ExitStack

import concourse.bass as bass
import concourse.mybir as mybir
from concourse.bass_utils import run_bass_kernel_spmd

F32 = mybir.dt.float32
BF16 = mybir.dt.bfloat16
FP8 = mybir.dt.float8e4
ALU = mybir.AluOpType
AF = mybir.ActivationFunctionType

HID = 1024
INTER = 2048
STATE = 128
NH = 32
PH = 64
KCONV = 4
Q = 256
CONV = 2304
DIN = 4384
FC = HID // 128
PC = INTER // 128
CC = CONV // 128
NBLK_IN = 35
SEQLEN = 4096
NSEQ = 2
NCHUNK = SEQLEN // Q
RAWW = Q + 3
NEG_BIG = -1.0e30


class Prog:

    def __init__(self):
        self.ops = []
        self.counts = {}

    def op(self, eng, fn, waits=(), inc=None, inc_k=1):
        wm = {}
        for t in waits:
            if t is None:
                continue
            s, v = t
            if v > wm.get(s, -1):
                wm[s] = v
        w = tuple(wm.items())
        tok = None
        if inc is not None:
            self.counts[inc] = self.counts.get(inc, 0) + inc_k
            tok = (inc, self.counts[inc])
        self.ops.append((eng, fn, w, inc, inc_k))
        return tok

    def tok(self, sem):
        return (sem, self.counts.get(sem, 0))

    def emit(self, name, eng, sems):
        acq = {}
        for ename, fn, waits, inc, inc_k in self.ops:
            if ename != name:
                continue
            for (s, v) in waits:
                if v > acq.get(s, 0):
                    eng.wait_ge(sems[s], v)
                    acq[s] = v
            if fn is None:
                continue
            inst = fn(eng)
            if inc is not None:
                inst.then_inc(sems[inc], inc_k)


STOP_PHASE = None


def build_program(n_rounds=NSEQ * NCHUNK):
    nc = bass.Bass("TRN2", target_bir_lowering=False)

    xin = nc.dram_tensor("xin", [NSEQ * NCHUNK, 128, FC * Q], F32, kind="ExternalInput")
    win = nc.dram_tensor("win", [NBLK_IN, 128, FC * 128], F32, kind="ExternalInput")
    wout = nc.dram_tensor("wout", [FC, 128, PC * 128], F32, kind="ExternalInput")
    wffn = nc.dram_tensor("wffn", [HID, HID], BF16, kind="ExternalInput")
    convw = nc.dram_tensor("convw", [128, CC * KCONV], F32, kind="ExternalInput")
    convb = nc.dram_tensor("convb", [128, CC], F32, kind="ExternalInput")
    dtb = nc.dram_tensor("dtb", [NH, 1], F32, kind="ExternalInput")
    alog = nc.dram_tensor("alog", [NH, 1], F32, kind="ExternalInput")
    dvec = nc.dram_tensor("dvec", [128, PC], F32, kind="ExternalInput")
    nwv = nc.dram_tensor("nwv", [128, PC], F32, kind="ExternalInput")
    fbias = nc.dram_tensor("fbias", [128, FC], BF16, kind="ExternalInput")
    sin_t = nc.dram_tensor("sin", [1, 1], F32, kind="ExternalInput")
    skk_t = nc.dram_tensor("skk", [1, 1], F32, kind="ExternalInput")
    idn_t = nc.dram_tensor("idn", [128, 128], F32, kind="ExternalInput")
    mka_t = nc.dram_tensor("mka", [128, 2 * Q], F32, kind="ExternalInput")
    yout = nc.dram_tensor("yout", [NSEQ * NCHUNK, 128, FC * Q], F32, kind="ExternalOutput")
    win_hl = nc.dram_tensor("win_hl", [NBLK_IN, 128, 2 * FC * 128], BF16)
    wout_hl = nc.dram_tensor("wout_hl", [FC, 128, 2 * PC * 128], BF16)

    es = ExitStack()

    def S(name, shape, dt=F32):
        return es.enter_context(nc.sbuf_tensor(name, shape, dt))

    x0b = [S(f"x0_{j}", [128, FC * Q]) for j in range(3)]
    raw = S("raw", [128, CC * RAWW])
    xbc = S("xbc", [128, CC * Q])
    ytl = S("ytl", [128, PC * Q])
    dtr = S("dtr", [NH, Q])
    dts = S("dts", [NH, Q])
    seg = S("seg", [NH, Q])
    dcy = S("dcy", [NH, Q])
    sp1 = S("sp1", [NH, Q])
    msk2 = S("msk2", [NH, 6 * Q])
    segT = S("segT", [128, 2 * NH])
    dtT = S("dtT", [128, 2 * NH])
    dcyT = S("dcyT", [128, 2 * NH])
    btok = S("btok", [128, 2 * 128])
    cdc = S("cdc", [128, NH])
    cdr = S("cdr", [1, NH])
    xdtT = [S(f"xdtT{j}", [128, INTER]) for j in range(2)]
    xddT = [S(f"xddT{j}", [128, INTER]) for j in range(2)]
    mtb = S("mtb", [128, 12 * Q])
    segB = S("segB", [128, 6 * Q])
    csg = S("csg", [128, 6 * Q])
    prevb = [S(f"prev{j}", [128, INTER]) for j in range(2)]
    winb = [S(f"winb{j}", [128, 2 * FC * 128], BF16) for j in range(3)]
    woutb = [S(f"woutb{j}", [128, 2 * PC * 64], BF16) for j in range(2)]
    wf8 = S("wf8", [128, FC * HID], FP8)
    xb16 = S("xb16", [128, 4 * Q], BF16)
    xsp = S("xsp", [128, 2 * FC * Q], BF16)
    xq8 = S("xq8", [128, FC * Q], FP8)
    scr = S("scr", [128, 2 * Q])
    idn = S("idn_s", [128, 128])
    mka = S("mka_s", [128, 2 * Q])
    onec = S("onec", [128, 1])
    ones32 = S("ones32", [NH, 128])
    oner = ones32.ap()[0:1, :]
    cc1 = S("cc1", [128, 1])
    ccol = S("ccol", [1, 8])
    dtbc = S("dtbc", [NH, 1])
    acol = S("acol", [NH, 1])
    dvc = S("dvc", [128, PC])
    nwc = S("nwc", [128, PC])
    inwc = S("inwc", [128, PC])
    fbc = S("fbc", [128, FC], BF16)
    cwc = S("cwc", [128, CC * KCONV])
    cbc = S("cbc", [128, CC])
    sccol = S("sccol", [128, 2])
    fbc32 = S("fbc32", [128, FC])
    sct = S("sct", [1, 4])

    rows = None
    psBIG = es.enter_context(nc.psum_tensor("psBIG", [128, 2048], F32))
    psCB = es.enter_context(nc.psum_tensor("psCB", [128, 512], F32))
    psSM = es.enter_context(nc.psum_tensor("psSM", [128, 512], F32))
    psY0 = es.enter_context(nc.psum_tensor("psY0", [128, 512], F32))
    psY1 = es.enter_context(nc.psum_tensor("psY1", [128, 512], F32))
    psY = [psY0, psY1]

    sem_names = ["SC", "SF0", "SF1", "SH0", "SH1", "SW0", "SW1", "SW2", "SWO0", "SWO1",
                 "SX0", "SX1", "SX2", "SL0", "SL1", "SO", "ST", "SV", "SS"]
    sems = {n: es.enter_context(nc.semaphore(n)) for n in sem_names}

    rows = bass.AP(mtb, 4 * Q, [[12 * Q, 1], [1, 3 * Q]])

    P = Prog()

    def dma(q, out_ap, in_ap, sem, waits=()):
        return P.op(q, lambda e, o=out_ap, i=in_ap: e.dma_start(out=o, in_=i),
                    waits=waits, inc=sem, inc_k=16)

    def ten(fn, waits=()):
        return P.op("tensor", fn, waits=waits, inc="ST")

    def vec(fn, waits=()):
        return P.op("vector", fn, waits=waits, inc="SV")

    def sca(fn, waits=()):
        return P.op("scalar", fn, waits=waits, inc="SS")

    c_eps5 = ccol.ap()[0:1, 0:1]
    c_eps6 = ccol.ap()[0:1, 1:2]
    c_i1024 = ccol.ap()[0:1, 2:3]
    c_i2048 = ccol.ap()[0:1, 3:4]
    c_nhalf = ccol.ap()[0:1, 4:5]

    dma("sync", idn[:], idn_t.ap()[:], "SC")
    dma("sync", mka[:], mka_t.ap()[:], "SC")
    dma("sync", cwc[:], convw.ap()[:], "SC")
    dma("sync", cbc[:], convb.ap()[:], "SC")
    dma("sync", dtbc[:], dtb.ap()[:], "SC")
    dma("sync", sp1[0:NH, 0:1], alog.ap()[:], "SC")
    dma("sync", dvc[:], dvec.ap()[:], "SC")
    dma("sync", nwc[:], nwv.ap()[:], "SC")
    dma("sync", fbc[:], fbias.ap()[:], "SC")
    dma("sync", sct[0:1, 0:1], sin_t.ap()[:], "SC")
    t_cst = dma("sync", sct[0:1, 1:2], skk_t.ap()[:], "SC")
    t_idn = t_mka = t_cw = t_cb2 = t_dtb = t_alog = t_cst
    t_dv = t_nw = t_fb = t_sin = t_skk = t_cst

    vec(lambda e: e.memset(onec[:], 1.0))
    vec(lambda e: e.memset(ones32[:], 1.0))
    vec(lambda e: e.memset(cc1[:], 1.0))
    vec(lambda e: e.memset(ccol[0:1, 0:1], 1e-5))
    vec(lambda e: e.memset(ccol[0:1, 1:2], 1e-6))
    vec(lambda e: e.memset(ccol[0:1, 2:3], 1.0 / HID))
    vec(lambda e: e.memset(ccol[0:1, 3:4], 1.0 / INTER))
    vec(lambda e: e.memset(ccol[0:1, 4:5], -0.5))
    t_pv0 = vec(lambda e: e.memset(prevb[0][:], 0.0))
    t_ms = vec(lambda e: e.memset(raw[:], 0.0))

    t_a1 = sca(lambda e: e.activation(acol[:], sp1[0:NH, 0:1], AF.Exp), waits=[t_alog])
    t_fb32 = sca(lambda e: e.copy(fbc32[:], fbc[:]), waits=[t_fb])
    t_a2 = vec(lambda e: e.tensor_scalar(acol[:], acol[:], -1.0, None, ALU.mult),
               waits=[t_a1])
    t_inw = vec(lambda e: e.reciprocal(inwc[:], nwc[:]), waits=[t_cst])
    t_r1 = vec(lambda e: e.reciprocal(sct[0:1, 2:3], sct[0:1, 0:1]), waits=[t_sin])
    t_r2 = vec(lambda e: e.tensor_tensor(sct[0:1, 3:4], sct[0:1, 0:1], sct[0:1, 1:2],
                                         ALU.mult), waits=[t_skk])
    t_bc0 = ten(lambda e: e.matmul(psSM[:, 0:2], oner, sct[0:1, 2:4],
                                   start=True, stop=True), waits=[t_r1, t_r2])
    t_scc = vec(lambda e: e.tensor_copy(sccol[:], psSM[:, 0:2]), waits=[t_bc0])

    tq_prev = [None, None]
    for kc in range(FC):
        half = kc % 2
        dst = xsp[:, half * HID:(half + 1) * HID]
        in_ap = bass.AP(wffn, kc * 128 * HID, [[HID, 128], [1, HID]])
        tl = dma("sync", dst, in_ap, f"SF{half}", waits=[tq_prev[half]])
        tq_prev[half] = vec(
            lambda e, kc=kc, dst=dst: e.tensor_scalar(
                wf8[:, kc * HID:(kc + 1) * HID], dst, sccol.ap()[:, 0:1], None,
                ALU.mult),
            waits=[tl, t_scc])
    t_wf8 = tq_prev[0], tq_prev[1]

    t_wprep_sp = [tq_prev[0], tq_prev[1]]
    t_wprep_hist = [[tq_prev[0]], [tq_prev[1]]]
    t_wprep_st = None
    for blk in range(NBLK_IN + FC):
        wb = blk % 2
        if blk < NBLK_IN:
            in_ap = bass.AP(win, blk * 128 * FC * 128, [[FC * 128, 128], [1, FC * 128]])
        else:
            in_ap = bass.AP(wout, (blk - NBLK_IN) * 128 * PC * 128,
                            [[PC * 128, 128], [1, 2 * PC * 64]])
        w = FC * 128 if blk < NBLK_IN else 2 * PC * 64
        stg = ytl[:, wb * 2048:wb * 2048 + w]
        xo_ = 0 if (w == 2048 or wb == 0) else 1024
        tl = dma("sync", stg, in_ap, f"SWO{wb}", waits=[t_wprep_sp[wb]])
        th = sca(lambda e, stg=stg, w=w, xo_=xo_: e.copy(
            xsp[:, xo_:xo_ + w], stg),
            waits=[tl, tq_prev[0], tq_prev[1]] + list(t_wprep_hist[wb]) +
                  (list(t_wprep_hist[1 - wb]) if w == 2048 else []))
        tl2 = vec(lambda e, stg=stg, w=w, xo_=xo_: e.tensor_tensor(
            xsp[:, 2048 + xo_:2048 + xo_ + w], stg, xsp[:, xo_:xo_ + w],
            ALU.subtract), waits=[th])
        t_wprep_sp[wb] = tl2
        if blk < NBLK_IN:
            oh = bass.AP(win_hl, blk * 128 * 2 * FC * 128,
                         [[2 * FC * 128, 128], [1, FC * 128]])
            ol = bass.AP(win_hl, blk * 128 * 2 * FC * 128 + FC * 128,
                         [[2 * FC * 128, 128], [1, FC * 128]])
        else:
            b2 = blk - NBLK_IN
            oh = bass.AP(wout_hl, b2 * 128 * 2 * PC * 128,
                         [[2 * PC * 128, 128], [1, 2 * PC * 64]])
            ol = bass.AP(wout_hl, b2 * 128 * 2 * PC * 128 + PC * 128,
                         [[2 * PC * 128, 128], [1, 2 * PC * 64]])
        t_oh = dma("scalar", oh, xsp[:, xo_:xo_ + w], f"SH{wb}", waits=[tl2])
        t_wprep_st = dma("gpsimd", ol, xsp[:, 2048 + xo_:2048 + xo_ + w],
                         f"SL{wb}", waits=[tl2])
        t_wprep_hist[wb] = [t_oh, t_wprep_st]
    t_wprep_done = [P.tok("SH0"), P.tok("SH1"), P.tok("SL0"), P.tok("SL1")]
    free_xsp = t_wprep_st

    free_winb = [None, None, None]
    free_woutb = [None, None]
    bigfree = [None, None, None, None]
    t_halo = None
    free_xbc = None
    free_ytl = None
    free_x0 = None
    t_psy_free = [t_scc, t_scc]
    t_psSM_free = [t_scc]
    psCB_free = None
    scr_free = [None, None]
    t_xb16_free = [None, None, None, None]
    t_out = None

    def win_dma(buf, blk, waits):
        in_ap = bass.AP(win_hl, blk * 128 * 2 * FC * 128,
                        [[2 * FC * 128, 128], [1, 2 * FC * 128]])
        return dma("sync", winb[buf][:, :], in_ap,
                   f"SW{buf}", waits=list(waits) + t_wprep_done)

    def wout_dma(buf, blk, half, waits):
        in_ap = bass.AP(wout_hl, blk * 128 * 2 * PC * 128 + half * PC * 64,
                        [[2 * PC * 128, 128], [PC * 128, 2], [1, PC * 64]])
        return dma("sync", woutb[buf][:, :], in_ap, f"SWO{buf}",
                   waits=list(waits) + t_wprep_done)

    def x0_dma(r, waits):
        in_ap = bass.AP(xin, r * 128 * FC * Q, [[FC * Q, 128], [1, FC * Q]])
        return dma("gpsimd", x0b[r % 3][:], in_ap, f"SX{r % 3}", waits=waits)

    def out_dma(r, waits):
        out_ap = bass.AP(yout, r * 128 * FC * Q, [[FC * Q, 128], [1, FC * Q]])
        return dma("gpsimd", out_ap, ytl[:, 0:FC * Q], "SO", waits=waits)

    t_x0 = x0_dma(0, [])

    x0_next_waits = []
    x0_tok = {0: t_x0}
    if NSEQ * NCHUNK > 1:
        x0_tok[1] = x0_dma(1, [])
    halo_tok = {0: t_ms}
    B_state = {}
    dtr_free = None
    prev_writer = [t_pv0, None]
    prev_reader_last = [None, None]

    R_chain = {}
    segfam_free = []
    NPAIR = NH // 2
    LAG = 2
    DEP = LAG + 1
    segB_free = [None] * DEP
    mtb_free = [[], [], []]
    csg_free = [None] * DEP
    t_msk_free = [None] * DEP

    R_conv = {}
    R_silu = {}
    D6_pre = {}

    def emit_silus_for(r):
        tap_toks = R_conv[r]
        toks = [None] * CC
        for cci in [0, 1, 2, 3, 16, 17] + list(range(4, 16)):
            toks[cci] = sca(lambda e, cci=cci: e.activation(
                xbc[:, cci * Q:(cci + 1) * Q], xbc[:, cci * Q:(cci + 1) * Q],
                AF.Silu, bias=cbc.ap()[:, cci:cci + 1]),
                waits=[tap_toks[cci], t_cst])
        R_silu[r] = toks

    def conv_generator(r, ypart):
        evac_ = B_state[r]['evac']
        conv_toks = [None] * CC
        R_conv[r] = conv_toks
        for cci in [16, 17] + list(range(16)):
            base = cci * RAWW
            assert evac_[cci] is not None
            w0 = [evac_[cci], halo_tok[r], t_cst, P.tok("ST")]
            if cci < 16:
                w0.append(ypart[cci])
            else:
                w0.append(P.tok("SV"))
            tt = vec(lambda e, cci=cci, base=base: e.tensor_scalar(
                xbc[:, cci * Q:(cci + 1) * Q], raw[:, base:base + Q],
                cwc.ap()[:, cci * KCONV:cci * KCONV + 1], None, ALU.mult),
                waits=w0)
            for k in range(1, KCONV):
                tt = vec(lambda e, cci=cci, base=base, k=k: e.scalar_tensor_tensor(
                    xbc[:, cci * Q:(cci + 1) * Q], raw[:, base + k:base + k + Q],
                    cwc.ap()[:, cci * KCONV + k:cci * KCONV + k + 1],
                    xbc[:, cci * Q:(cci + 1) * Q], ALU.mult, ALU.add), waits=[tt])
            conv_toks[cci] = tt
            yield
        halo_ap = bass.AP(raw, 0, [[CC * RAWW, 128], [RAWW, CC], [1, 3]])
        if (r + 1) % NCHUNK == 0:
            halo_tok[r + 1] = vec(lambda e, a=halo_ap: e.memset(a, 0.0),
                                  waits=[P.tok("SV")])
        else:
            src = bass.AP(raw, Q, [[CC * RAWW, 128], [RAWW, CC], [1, 3]])
            halo_tok[r + 1] = vec(lambda e, d=halo_ap, s=src: e.tensor_copy(d, s),
                                  waits=[P.tok("SV")])


    def emit_dt_chain(r):
        nonlocal dtr_free
        t_dtr_ = B_state[r]['t_dtr']
        w0 = list(segfam_free)
        t_d = sca(lambda e: e.activation(sp1[0:NH, :], dtr[:], AF.Abs,
                                         bias=dtbc.ap()[:]),
                  waits=[t_dtr_, t_cst] + w0)
        t_d2 = sca(lambda e: e.activation(sp1[0:NH, :], sp1[0:NH, :], AF.Exp,
                                          scale=-1.0), waits=[t_d])
        t_sp = sca(lambda e: e.activation(sp1[0:NH, :], sp1[0:NH, :], AF.Ln,
                                          bias=cc1.ap()[0:NH, 0:1]), waits=[t_d2])
        t_rl = sca(lambda e: e.activation(dts[:], dtr[:], AF.Relu,
                                          bias=dtbc.ap()[:]), waits=w0)
        dtr_free = t_rl
        t_dt = vec(lambda e: e.tensor_tensor(dts[:], dts[:], sp1[0:NH, :],
                                             ALU.add), waits=[t_sp, t_rl] + w0)
        t_dta = vec(lambda e: e.tensor_scalar(sp1[0:NH, :], dts[:], acol.ap()[:],
                                              None, ALU.mult), waits=[t_dt, t_a2])
        t_seg = vec(lambda e: e.tensor_tensor_scan(seg[:], sp1[0:NH, :],
                                                   sp1[0:NH, :], 0.0,
                                                   ALU.add, ALU.bypass),
                    waits=[t_dta])
        t_dcy = sca(lambda e: e.activation(dcy[:], seg[:], AF.Exp,
                                           bias=seg.ap()[:, Q - 1:Q], scale=-1.0),
                    waits=[t_seg])
        cbfree = psCB_free
        tr = []
        for si, (srcT, stok, dstT) in enumerate(
                [(seg, t_seg, segT), (dts, t_dt, dtT), (dcy, t_dcy, dcyT)]):
            for jb in range(2):
                col = (si * 2 + jb) * 32
                tk = ten(lambda e, srcT=srcT, jb=jb, col=col: e.transpose(
                    psCB[:, col:col + NH], srcT[:, jb * 128:(jb + 1) * 128],
                    idn[0:NH, 0:NH]),
                    waits=[stok, t_cst, cbfree])
                tr.append((tk, dstT, col, jb))
        t_cd1 = ten(lambda e: e.transpose(psCB[0:1, 192:192 + NH], seg[:, Q - 1:Q],
                                          idn[0:NH, 0:NH]), waits=[t_seg])
        ev_tr = []
        for tk, dstT, col, jb in tr:
            ev_tr.append(vec(lambda e, dstT=dstT, col=col, jb=jb: e.tensor_copy(
                dstT[:, jb * NH:(jb + 1) * NH], psCB[:, col:col + NH]),
                waits=[tk, t_cd1]))
        t_cd2 = sca(lambda e: e.activation(cdr[:], psCB[0:1, 192:192 + NH], AF.Exp),
                    waits=[t_cd1])
        t_cd3 = ten(lambda e: e.matmul(psCB[:, 224:224 + NH], oner, cdr[:],
                                       start=True, stop=True),
                    waits=[t_cd2] + ev_tr)
        t_cd = sca(lambda e: e.copy(cdc[:], psCB[:, 224:224 + NH]),
                   waits=[t_cd3])
        R_chain[r] = dict(t_dt=t_dt, t_seg=t_seg, t_dcy=t_dcy,
                          t_segT=ev_tr[1], t_dtT=ev_tr[3], t_dcyT=ev_tr[5],
                          t_cd=t_cd, t_cd3=t_cd3, ev_last=ev_tr[-1],
                          tr_last=tr[-1][0])

    def emit_btok(r):
        ch = R_chain[r]
        t_bt = []
        for jb in range(2):
            tk = ten(lambda e, jb=jb: e.transpose(
                psCB[:, 256 + jb * 128:256 + (jb + 1) * 128],
                xbc[:, 16 * Q + jb * 128:16 * Q + jb * 128 + 128], idn[:]),
                waits=[R_silu[r][16], ch['t_cd3'], ch['t_cd'], ch['ev_last']])
            t_bt.append(sca(lambda e, jb=jb: e.copy(
                btok[:, jb * 128:(jb + 1) * 128],
                psCB[:, 256 + jb * 128:256 + (jb + 1) * 128]), waits=[tk]))
        ch['bt'] = t_bt

    def B_prologue(r):
        x0r = x0b[r % 3]
        blocks = [NBLK_IN - 1] + list(range(16, NBLK_IN - 1))
        w_tok = {}
        w_tok[blocks[0]] = win_dma(0, blocks[0], [free_winb[0]])
        w_tok[blocks[1]] = win_dma(1, blocks[1], [free_winb[1]])
        w_tok[blocks[2]] = win_dma(2, blocks[2], [free_winb[2]])
        th = sca(lambda e, x0=x0r: e.copy(xsp[:, 0:2048], x0[:]),
                 waits=[x0_tok[r], free_xsp] + t_wprep_done)
        t_xsp = vec(lambda e, x0=x0r: e.tensor_tensor(
            xsp[:, 2048:4096], x0[:], xsp[:, 0:2048], ALU.subtract),
            waits=[th, free_xsp])
        return blocks, w_tok, t_xsp

    def B_generator(r, blocks, w_tok, t_xsp):
        nonlocal dtr_free, free_xsp
        evac = [None] * CC
        st_ = {'evac': evac}
        B_state[r] = st_
        for bi, blk in enumerate(blocks):
            bs = 32 if blk == NBLK_IN - 1 else 128
            bank = bi % 4
            buf = bi % 3
            mm_w = [w_tok[blk], t_xsp, bigfree[bank]]
            last = None
            for kc in range(FC):
                for ph, (wo, xo) in enumerate(((0, 0), (0, 2048), (1024, 0))):
                    def mmf(e, buf=buf, kc=kc, bs=bs, bank=bank, wo=wo, xo=xo,
                            st=(kc == 0 and ph == 0),
                            sp=(kc == FC - 1 and ph == 2)):
                        return e.matmul(psBIG[0:bs, bank * 512:bank * 512 + Q],
                                        winb[buf][:, wo + kc * bs:wo + (kc + 1) * bs],
                                        xsp[:, xo + kc * Q:xo + (kc + 1) * Q],
                                        start=st, stop=sp)
                    last = ten(mmf, waits=mm_w if (kc == 0 and ph == 0) else ())
            if bi == len(blocks) - 1:
                free_xsp = last
            free_winb[buf] = last
            if bi + 3 < len(blocks):
                w_tok[blocks[bi + 3]] = win_dma(buf, blocks[bi + 3], [last])
            if blk < NBLK_IN - 1:
                cci = blk - 16
                tv = sca(lambda e, cci=cci, bank=bank: e.copy(
                    raw[:, cci * RAWW + 3:cci * RAWW + 3 + Q],
                    psBIG[:, bank * 512:bank * 512 + Q]),
                    waits=[last, halo_tok[r]])
                evac[cci] = tv
            else:
                tv = sca(lambda e, bank=bank: e.copy(
                    dtr[:], psBIG[0:NH, bank * 512:bank * 512 + Q]),
                    waits=[last, dtr_free])
                st_['t_dtr'] = tv
            bigfree[bank] = tv
            yield

    for r in range(n_rounds):
        c = r % NCHUNK
        seq_start = (c == 0)
        x0 = x0b[r % 3]
        if r + 2 < n_rounds:
            x0_tok[r + 2] = x0_dma(r + 2, waits=list(x0_next_waits))

        if r == 0:
            for _ in B_generator(0, *B_prologue(0)):
                pass
            emit_dt_chain(0)
            for _ in conv_generator(0, [None] * PC):
                pass
            emit_silus_for(0)
            emit_btok(0)
        if r + 1 < n_rounds:
            bgen = B_generator(r + 1, *B_prologue(r + 1))
        else:
            bgen = None
        evac = B_state[r]['evac']
        t_dtr = B_state[r]['t_dtr']
        prev_cur = prevb[r % 2]
        prev_new = prevb[(r + 1) % 2]
        if seq_start and r > 0:
            t_prev_ok = vec(lambda e, pv=prev_cur: e.memset(pv[:], 0.0),
                            waits=[prev_writer[r % 2], prev_reader_last[r % 2]])
        else:
            t_prev_ok = prev_writer[r % 2]

        if STOP_PHASE == "B":
            break
        ch = R_chain[r]
        t_dt, t_seg, t_dcy = ch["t_dt"], ch["t_seg"], ch["t_dcy"]
        t_segT, t_dtT, t_dcyT = ch["t_segT"], ch["t_dtT"], ch["t_dcyT"]
        t_cd = ch["t_cd"]
        t_bt = ch["bt"]

        if STOP_PHASE == "D1":
            break
        conv_toks = R_silu[r]

        def emit_d6block(rr):
            chb = R_chain[rr]
            tcd_b, tbt_b = chb["t_cd"], chb["bt"]
            tdtT_b, tdcyT_b = chb["t_dtT"], chb["t_dcyT"]
            convt = R_silu[rr]
            grp_done = [[None] * 4 for _ in range(2)]
            tr0 = [None] * 4
            for g in range(4):
                trs = None
                for fi in range(4):
                    fc_i = g * 4 + fi
                    wts = [convt[fc_i], t_idn, bigfree[g]]
                    def trf(e, fc_i=fc_i):
                        return e.transpose(
                            psBIG[:, fc_i * 128:(fc_i + 1) * 128],
                            xbc[:, fc_i * Q + 0 * 128: fc_i * Q + 0 * 128 + 128],
                            idn[:])
                    trs = ten(trf, waits=wts)
                tr0[g] = trs
            t_cbm = []
            for jb in range(2):
                t_cbm.append(ten(lambda e, jb=jb: e.matmul(
                    psCB[:, jb * Q:(jb + 1) * Q],
                    xbc[:, 16 * Q + jb * 128:16 * Q + jb * 128 + 128],
                    xbc[:, 17 * Q:18 * Q], start=True, stop=True),
                    waits=[convt[16], convt[17], psCB_free,
                           tbt_b[0], tbt_b[1], tcd_b]))
            for jb in range(2):
                for g in range(4):
                    if jb == 0:
                        trs = tr0[g]
                    else:
                        trs = None
                        for fi in range(4):
                            fc_i = g * 4 + fi
                            def trf(e, fc_i=fc_i, jb=jb):
                                return e.transpose(
                                    psBIG[:, fc_i * 128:(fc_i + 1) * 128],
                                    xbc[:, fc_i * Q + jb * 128: fc_i * Q + jb * 128 + 128],
                                    idn[:])
                            trs = ten(trf, waits=[grp_done[0][g]] if fi == 0 else ())
                    big3 = bass.AP(psBIG, g * 512, [[2048, 128], [PH, 8], [1, PH]])
                    xdt3 = bass.AP(xdtT[jb], g * 512, [[INTER, 128], [PH, 8], [1, PH]])
                    dtb3 = dtT.ap()[:, jb * NH + g * 8:jb * NH + g * 8 + 8].broadcast_to(
                        [128, 8, PH])
                    tx = vec(lambda e, a=xdt3, b=big3, c=dtb3: e.tensor_tensor(
                        a, b, c, ALU.mult), waits=[trs, tdtT_b, grp_done[1][g] if jb == 0 else None])
                    xdd3 = bass.AP(xddT[jb], g * 512, [[INTER, 128], [PH, 8], [1, PH]])
                    dcb3 = dcyT.ap()[:, jb * NH + g * 8:jb * NH + g * 8 + 8].broadcast_to(
                        [128, 8, PH])
                    td = vec(lambda e, a=xdd3, b=xdt3, c=dcb3: e.tensor_tensor(
                        a, b, c, ALU.mult), waits=[tdcyT_b, tx])
                    grp_done[jb][g] = td
            t_xdt_done = P.tok("SV")
            t_st = [None] * 4
            for g in range(4):
                last = None
                for jb in range(2):
                    def stf(e, g=g, jb=jb):
                        return e.matmul(psBIG[:, g * 512:(g + 1) * 512],
                                        btok[:, jb * 128:(jb + 1) * 128],
                                        xddT[jb][:, g * 512:(g + 1) * 512],
                                        start=(jb == 0), stop=(jb == 1))
                    last = ten(stf, waits=[tbt_b[jb], grp_done[0][g],
                                           grp_done[1][g]])
                t_st[g] = last
            return t_cbm, t_xdt_done, t_st

        if r in D6_pre:
            t_cbm, t_xdt_done, t_st = D6_pre.pop(r)
        else:
            t_cbm, t_xdt_done, t_st = emit_d6block(r)


        t_scan = None
        scan_bank_done = [None] * 4

        def emit_scan(h):
            nonlocal t_scan
            g = h // 8
            wts = [t_st[g], t_cd]
            if h == 0:
                wts += [prev_reader_last[(r + 1) % 2], t_prev_ok]
            def scf(e, h=h, pc_=prev_cur, pn_=prev_new):
                return e.scalar_tensor_tensor(
                    pn_[:, h * PH:(h + 1) * PH], pc_[:, h * PH:(h + 1) * PH],
                    cdc.ap()[:, h:h + 1], psBIG[:, h * PH:(h + 1) * PH],
                    ALU.mult, ALU.add)
            t_scan = vec(scf, waits=wts)
            if h % 8 == 7:
                scan_bank_done[g] = t_scan
                bigfree[g] = t_scan

        t_y_parts = [None] * PC
        cgen = conv_generator(r + 1, t_y_parts) if r + 1 < n_rounds else None
        t_psy_evac = list(t_psy_free)
        t_seg_copy = [None] * (NPAIR + 1)
        t_msk_tok = [None] * (NPAIR + 1)
        pair_state = {}

        def emit_msk(p):
            if p >= NPAIR:
                return
            mb_ = p % DEP
            tm0 = sca(lambda e, p=p, mb_=mb_: e.activation(
                msk2[:, mb_ * 2 * Q:mb_ * 2 * Q + Q], seg[:],
                AF.Copy, scale=idn.ap()[0:NH, 2 * p:2 * p + 1]),
                waits=[t_seg, t_cst, t_msk_free[mb_]])
            tm1 = sca(lambda e, p=p, mb_=mb_: e.activation(
                msk2[:, mb_ * 2 * Q + Q:mb_ * 2 * Q + 2 * Q], seg[:],
                AF.Copy, scale=idn.ap()[0:NH, 2 * p + 1:2 * p + 2]),
                waits=[tm0])
            t_msk_tok[p] = tm1

        for p in range(LAG):
            emit_msk(p)

        for p in range(NPAIR + LAG):
            buf = p % DEP
            if p < NPAIR:
                emit_msk(p + LAG)
                tsg = ten(lambda e, buf=buf: e.matmul(
                    psSM[:, 0:2 * Q], ones32[:],
                    msk2[:, buf * 2 * Q:(buf + 1) * 2 * Q],
                    start=True, stop=True),
                          waits=([t_msk_tok[p]] +
                                 ([t_seg_copy[p - 1]] if p > 0 else t_psSM_free)))
                t_msk_free[buf] = tsg
                tcp = sca(lambda e, buf=buf: e.copy(
                    segB[:, buf * 2 * Q:(buf + 1) * 2 * Q], psSM[:, 0:2 * Q]),
                    waits=[tsg, segB_free[buf]])
                t_seg_copy[p] = tcp
                tdfs = []
                tmts2 = []
                for hh in range(2):
                    h = 2 * p + hh
                    base2 = buf * 4 * Q + hh * 2 * Q
                    for jb in range(2):
                        sl = base2 + jb * Q
                        sb = buf * 2 * Q + hh * Q
                        tdf = vec(lambda e, sl=sl, sb=sb, jb=jb, h=h:
                                  e.scalar_tensor_tensor(
                                      mtb[:, sl:sl + Q], segB[:, sb:sb + Q],
                                      segT.ap()[:, jb * NH + h:jb * NH + h + 1],
                                      mka[:, jb * Q:(jb + 1) * Q],
                                      ALU.subtract, ALU.add),
                                  waits=[tcp, t_segT] + list(mtb_free[buf]))
                        tdfs.append(tdf)
                    tex = sca(lambda e, base2=base2: e.activation(
                        mtb[:, base2:base2 + 2 * Q], mtb[:, base2:base2 + 2 * Q],
                        AF.Exp), waits=[tdfs[-2], tdfs[-1]])
                    tmt = vec(lambda e, base2=base2: e.tensor_tensor(
                        mtb[:, base2:base2 + 2 * Q], mtb[:, base2:base2 + 2 * Q],
                        psCB[:, 0:2 * Q], ALU.mult),
                        waits=[tex, t_cbm[0], t_cbm[1]])
                    tmts2.append(tmt)
                tmts = [tmts2[0], tmts2[0], tmts2[1], tmts2[1]]
                tse = sca(lambda e, buf=buf: e.activation(
                    segB[:, buf * 2 * Q:(buf + 1) * 2 * Q],
                    segB[:, buf * 2 * Q:(buf + 1) * 2 * Q], AF.Exp),
                    waits=tdfs)
                tcss = []
                for hh in range(2):
                    sb = buf * 2 * Q + hh * Q
                    tcs = vec(lambda e, buf=buf, hh=hh, sb=sb: e.tensor_tensor(
                        csg[:, sb:sb + Q], xbc[:, 17 * Q:18 * Q],
                        segB[:, sb:sb + Q], ALU.mult),
                        waits=[tse, conv_toks[17], csg_free[buf]])
                    tcss.append(tcs)
                segB_free[buf] = tcss[-1]
                pair_state[p] = (tmts, tcss)
            if p < 4:
                for hh8 in range(8):
                    emit_scan(p * 8 + hh8)
                if p == 3:
                    prev_writer[(r + 1) % 2] = t_scan
            pd = p - LAG
            if 0 <= pd < NPAIR:
                dbuf = pd % DEP
                tmts, tcss = pair_state.pop(pd)
                ps = psY[pd % 2]
                lastmm = None
                for hh in range(2):
                    h = 2 * pd + hh
                    tp = (0, PH * hh)
                    mm_w = [tmts[hh * 2], tmts[hh * 2 + 1]]
                    if hh == 0:
                        mm_w += [t_psy_evac[pd % 2], t_xdt_done]
                    for jb in range(2):
                        sl = dbuf * 4 * Q + (hh * 2 + jb) * Q
                        def ydf(e, h=h, jb=jb, ps=ps, tp=tp, hh=hh, sl=sl,
                                st=(jb == 0)):
                            return e.matmul(
                                ps[PH * hh:PH * hh + PH, 0:Q],
                                xdtT[jb][:, h * PH:(h + 1) * PH],
                                mtb[:, sl:sl + Q],
                                start=st, stop=False, tile_position=tp)
                        lastmm = ten(ydf, waits=mm_w if jb == 0 else ())
                    sb = dbuf * 2 * Q + hh * Q
                    def yof(e, h=h, ps=ps, tp=tp, hh=hh, sb=sb, pc_=prev_cur):
                        return e.matmul(ps[PH * hh:PH * hh + PH, 0:Q],
                                        pc_[:, h * PH:(h + 1) * PH],
                                        csg[:, sb:sb + Q],
                                        start=False, stop=True, tile_position=tp)
                    lastmm = ten(yof, waits=[tcss[hh], t_prev_ok])
                    prev_reader_last[r % 2] = lastmm
                mtb_free[dbuf] = [lastmm]
                csg_free[dbuf] = lastmm
                def evf(e, pc_i=pd, ps=ps):
                    return e.scalar_tensor_tensor(
                        ytl[:, pc_i * Q:(pc_i + 1) * Q],
                        xbc[:, pc_i * Q:(pc_i + 1) * Q],
                        dvc.ap()[:, pc_i:pc_i + 1], ps[:, 0:Q], ALU.mult, ALU.add)
                tev = vec(evf, waits=[lastmm, conv_toks[pd], free_ytl, t_cst])
                t_psy_evac[pd % 2] = tev
                t_y_parts[pd] = tev
            if bgen is not None and p >= 1:
                next(bgen, None)
        if bgen is not None:
            for _ in bgen:
                pass
        segfam_free = [P.tok("SV"), P.tok("SS"), P.tok("ST")]
        psCB_free = P.tok("SV")
        if r + 1 < n_rounds:
            emit_dt_chain(r + 1)
        t_psSM_free = [t_seg_copy[NPAIR - 1]]

        if STOP_PHASE == "D":
            break
        t_rms_mm = None
        th_e_a = sca(lambda e, x0=x0: e.copy(xsp[:, 0:1024], x0[:, 0:1024]),
                     waits=[free_xsp])
        t_xe_a = vec(lambda e, x0=x0: e.tensor_tensor(
            xsp[:, 2048:3072], x0[:, 0:1024], xsp[:, 0:1024], ALU.subtract),
            waits=[th_e_a, free_xsp])
        th_e_b = sca(lambda e, x0=x0: e.copy(xsp[:, 1024:2048], x0[:, 1024:2048]),
                     waits=[th_e_a])
        t_xsp_e = vec(lambda e, x0=x0: e.tensor_tensor(
            xsp[:, 3072:4096], x0[:, 1024:2048], xsp[:, 1024:2048],
            ALU.subtract), waits=[th_e_b, t_xe_a])
        for mb in range(PC):
            bank = mb % 4
            buf = mb % 3
            tw = win_dma(buf, mb, [free_winb[buf]])
            last = None
            for kc in range(FC):
                for ph, (wo, xo) in enumerate(((0, 0), (0, 2048), (1024, 0))):
                    def mmf(e, mb=mb, kc=kc, bank=bank, buf=buf, wo=wo, xo=xo,
                            st=(kc == 0 and ph == 0),
                            sp=(kc == FC - 1 and ph == 2)):
                        return e.matmul(psBIG[:, bank * 512:bank * 512 + Q],
                                        winb[buf][:, wo + kc * 128:wo + (kc + 1) * 128],
                                        xsp[:, xo + kc * Q:xo + (kc + 1) * Q],
                                        start=st, stop=sp)
                    if kc == 0 and ph == 0:
                        e_wts = [tw, bigfree[bank], t_xe_a]
                    elif mb == 0 and kc == 4 and ph == 0:
                        e_wts = [t_xsp_e]
                    else:
                        e_wts = ()
                    last = ten(mmf, waits=e_wts)
            free_winb[buf] = last
            free_x0 = last
            if mb == PC - 1:
                free_xsp = last
            par = mb % 2
            def emit_fstat(mbq):
                nonlocal t_rms_mm
                parq = mbq % 2
                t_sq_f = sca(lambda e, mbq=mbq, parq=parq: e.activation(
                    scr[:, parq * Q:(parq + 1) * Q], ytl[:, mbq * Q:(mbq + 1) * Q],
                    AF.Square, scale=inwc.ap()[:, mbq:mbq + 1]),
                    waits=[t_y_parts[mbq], scr_free[parq], t_inw])
                def smf_f(e, parq=parq, st=(mbq == 0), sp=(mbq == PC - 1)):
                    return e.matmul(psSM[0:1, 0:Q], onec[:],
                                    scr[:, parq * Q:(parq + 1) * Q],
                                    start=st, stop=sp)
                wts_f = [t_sq_f]
                if mbq == 0:
                    wts_f += t_psSM_free
                t_rms_mm = ten(smf_f, waits=wts_f)
                scr_free[parq] = t_rms_mm
            if mb >= 2:
                emit_fstat(mb - 2)
            tsl = sca(lambda e, bank=bank, par=par: e.activation(
                scr[:, par * Q:(par + 1) * Q],
                psBIG[:, bank * 512:bank * 512 + Q], AF.Silu),
                waits=[last, scr_free[par]])
            bigfree[bank] = tsl
            tmu = vec(lambda e, mb=mb, par=par: e.scalar_tensor_tensor(
                ytl[:, mb * Q:(mb + 1) * Q], ytl[:, mb * Q:(mb + 1) * Q],
                nwc.ap()[:, mb:mb + 1],
                scr[:, par * Q:(par + 1) * Q], ALU.mult, ALU.mult),
                waits=[tsl, t_y_parts[mb], t_nw])
            t_y_parts[mb] = tmu
            if mb == PC - 2:
                th_g0 = vec(lambda e: e.tensor_copy(
                    winb[1][:, 0:2048], ytl[:, 0:2048]),
                    waits=[free_winb[1], max(t_y_parts[0:8])])
            if mb == PC - 1:
                t_ysp0 = vec(lambda e: e.tensor_tensor(
                    winb[2][:, 0:2048], ytl[:, 0:2048], winb[1][:, 0:2048],
                    ALU.subtract), waits=[th_g0, free_winb[2]])
            if cgen is not None:
                next(cgen, None)
                if mb >= PC - 2:
                    next(cgen, None)
        if cgen is not None:
            for _ in cgen:
                pass
        emit_fstat(PC - 2)
        emit_fstat(PC - 1)

        if STOP_PHASE == "E":
            break
        t_warm_f = sca(lambda e: e.activation(scr[:, 0:1], scr[:, 0:1], AF.Exp),
                       waits=[scr_free[0]])
        scr_free[0] = t_warm_f
        t_f1 = sca(lambda e: e.activation(rows[:, Q:2 * Q], psSM[0:1, 0:Q], AF.Ln,
                                          scale=c_i2048, bias=c_eps6),
                   waits=[t_rms_mm])
        t_f2 = sca(lambda e: e.activation(rows[:, 0:Q], rows[:, Q:2 * Q], AF.Exp,
                                          scale=c_nhalf), waits=[t_f1])
        t_fb2 = ten(lambda e: e.matmul(psY0[:, 0:Q], oner, rows[:, 0:Q],
                                       start=True, stop=True),
                    waits=[t_f2, t_psy_evac[0]])
        t_rbc = sca(lambda e: e.copy(mtb[:, 0:Q], psY0[:, 0:Q]),
                    waits=[t_fb2] + list(mtb_free[0]))
        t_psSM_free = [t_f1]
        th_g1 = vec(lambda e: e.tensor_copy(
            winb[0][:, 0:2048], ytl[:, 2048:4096]),
            waits=[free_winb[0], max(t_y_parts[8:16])])
        t_ysp1 = vec(lambda e: e.tensor_tensor(
            xsp[:, 0:2048], ytl[:, 2048:4096], winb[0][:, 0:2048],
            ALU.subtract), waits=[th_g1, free_xsp])
        if r + 1 < n_rounds:
            emit_silus_for(r + 1)
            emit_btok(r + 1)
        t_warm_h = sca(lambda e: e.activation(scr[:, Q:Q + 1], scr[:, Q:Q + 1],
                                              AF.Exp),
                       waits=[scr_free[1]])
        scr_free[1] = t_warm_h

        if STOP_PHASE == "F":
            break
        seq_wh = [(h, w * 4 + mi) for h in range(2) for w in range(2)
                  for mi in range(4)]
        t_wo = {}
        t_wo[(seq_wh[0][1], seq_wh[0][0])] = wout_dma(
            seq_wh[0][1] % 2, seq_wh[0][1], seq_wh[0][0], [free_woutb[seq_wh[0][1] % 2]])
        t_wo[(seq_wh[1][1], seq_wh[1][0])] = wout_dma(
            seq_wh[1][1] % 2, seq_wh[1][1], seq_wh[1][0], [free_woutb[seq_wh[1][1] % 2]])
        g_evac = [None] * FC
        wv_last = None
        g_h0_last = None
        si = 0
        for half in range(2):
            t_ysp = t_ysp0 if half == 0 else t_ysp1
            rhs_hi = (lambda kc8: winb[1][:, kc8 * Q:(kc8 + 1) * Q]) if half == 0 \
                else (lambda kc8: winb[0][:, kc8 * Q:(kc8 + 1) * Q])
            rhs_lo = (lambda kc8: winb[2][:, kc8 * Q:(kc8 + 1) * Q]) \
                if half == 0 \
                else (lambda kc8: xsp[:, kc8 * Q:(kc8 + 1) * Q])
            for mb in range(FC):
                bank = mb % 4
                buf = mb % 2
                for kc8 in range(8):
                    for ph, (wo, rhs) in enumerate(
                            ((0, rhs_hi), (0, rhs_lo), (1024, rhs_hi))):
                        def mmf(e, mb=mb, kc8=kc8, bank=bank, buf=buf,
                                wo=wo, rhs=rhs,
                                st=(kc8 == 0 and ph == 0),
                                sp=(kc8 == 7 and ph == 2)):
                            return e.matmul(
                                psBIG[:, bank * 512:bank * 512 + Q],
                                woutb[buf][:, wo + kc8 * 128:wo + (kc8 + 1) * 128],
                                rhs(kc8),
                                start=st, stop=sp)
                        wts = []
                        if kc8 == 0 and ph == 0:
                            wts = [t_wo[(mb, half)], t_ysp, bigfree[bank]]
                        last = ten(mmf, waits=wts)
                if si + 2 < len(seq_wh):
                    nh, nmb = seq_wh[si + 2]
                    if (nmb, nh) not in t_wo:
                        t_wo[(nmb, nh)] = wout_dma(nmb % 2, nmb, nh, [last])
                si += 1
                wv_last = last
                par = mb % 2
                tg1 = vec(lambda e, bank=bank, par=par: e.tensor_tensor(
                    scr[:, par * Q:(par + 1) * Q],
                    psBIG[:, bank * 512:bank * 512 + Q], mtb[:, 0:Q], ALU.mult),
                    waits=[last, free_x0, t_rbc, scr_free[par]])
                tev = vec(lambda e, mb=mb, par=par, x0=x0: e.tensor_tensor(
                    x0[:, mb * Q:(mb + 1) * Q], x0[:, mb * Q:(mb + 1) * Q],
                    scr[:, par * Q:(par + 1) * Q], ALU.add),
                    waits=[tg1])
                scr_free[par] = tev
                bigfree[bank] = tev
                g_evac[mb] = tev
            if half == 0:
                g_h0_last = wv_last
        free_ytl = wv_last
        free_xsp = wv_last
        free_winb[0] = wv_last
        free_winb[1] = g_h0_last
        free_winb[2] = g_h0_last
        free_woutb[0] = free_woutb[1] = wv_last
        mtb_free[0] = [scr_free[0]]

        tb_last_ln = None

        def ln_phase(src_get, apply_out_get, eps_ap, inv_ap, t_src, extra_apply_w=(),
                     pair_t=None):
            nonlocal t_psSM_free
            sum_last = None
            for fci in range(FC):
                def smf(e, src=src_get(fci), st=(fci == 0), sp=(fci == FC - 1)):
                    return e.matmul(psSM[0:1, 0:Q], onec[:], src,
                                    start=st, stop=sp)
                wts = [t_src[fci]]
                if fci == 0:
                    wts += t_psSM_free
                sum_last = ten(smf, waits=wts)
            sq_last = None
            for fci in range(FC):
                par = fci % 2
                tsq = sca(lambda e, src=src_get(fci), par=par: e.activation(
                    scr[:, par * Q:(par + 1) * Q], src, AF.Square),
                    waits=[t_src[fci], scr_free[par]])
                def smf2(e, par=par, st=(fci == 0), sp=(fci == FC - 1)):
                    return e.matmul(psSM[0:1, Q:2 * Q], onec[:],
                                    scr[:, par * Q:(par + 1) * Q], start=st, stop=sp)
                sq_last = ten(smf2, waits=[tsq] + ([sum_last] if fci == 0 else []))
                scr_free[par] = sq_last
            t_m = vec(lambda e: e.tensor_scalar(
                rows[:, 2 * Q:3 * Q], psSM[0:1, 0:Q], inv_ap, None, ALU.mult),
                waits=[sq_last])
            t_m2 = vec(lambda e: e.tensor_tensor(
                rows[:, 0:Q], rows[:, 2 * Q:3 * Q], rows[:, 2 * Q:3 * Q],
                ALU.mult), waits=[t_m])
            t_var = vec(lambda e: e.scalar_tensor_tensor(
                rows[:, Q:2 * Q], psSM[0:1, 0 + Q:2 * Q], inv_ap,
                rows[:, 0:Q], ALU.mult, ALU.subtract),
                waits=[t_m2])
            t_l1 = sca(lambda e: e.activation(rows[:, Q:2 * Q], rows[:, Q:2 * Q],
                                              AF.Ln, bias=eps_ap), waits=[t_var])
            t_l2 = sca(lambda e: e.activation(rows[:, 0:Q], rows[:, Q:2 * Q],
                                              AF.Exp, scale=c_nhalf),
                       waits=[t_l1, t_var])
            t_mr = vec(lambda e: e.tensor_tensor(rows[:, Q:2 * Q],
                                                 rows[:, 2 * Q:3 * Q],
                                                 rows[:, 0:Q], ALU.mult),
                       waits=[t_l2, t_l1])
            tb1 = ten(lambda e: e.matmul(psY0[:, 0:Q], oner, rows[:, 0:Q],
                                         start=True, stop=True),
                      waits=[t_l2, t_psy_free_ln[0]])
            tb2 = ten(lambda e: e.matmul(psY1[:, 0:Q], oner,
                                         rows[:, Q:2 * Q],
                                         start=True, stop=True),
                      waits=[t_mr, t_psy_free_ln[1]])
            ap_toks = []
            src_t, s_str, dst_t, d_str = pair_t
            after_apply = extra_apply_w if callable(extra_apply_w) else None
            if after_apply is not None:
                extra_apply_w = ()
            for i2 in range(FC // 2):
                so = bass.AP(src_t, i2 * 2 * Q, [[s_str, 128], [Q, 2], [1, Q]])
                do = bass.AP(dst_t, i2 * 2 * Q, [[d_str, 128], [Q, 2], [1, Q]])
                y0b = bass.AP(psY0, 0, [[512, 128], [0, 2], [1, Q]])
                y1b = bass.AP(psY1, 0, [[512, 128], [0, 2], [1, Q]])
                v1 = vec(lambda e, do=do, so=so, y0b=y0b: e.tensor_tensor(
                    do, so, y0b, ALU.mult),
                    waits=[tb1, P.tok("SV"), P.tok("SS")] + list(extra_apply_w))
                v2 = vec(lambda e, do=do, y1b=y1b: e.tensor_tensor(
                    do, do, y1b, ALU.subtract), waits=[tb2, v1])
                ap_toks.append(v2)
                ap_toks.append(v2)
                if after_apply is not None:
                    after_apply(i2, v2)
            nonlocal tb_last_ln
            tb_last_ln = tb2
            t_psSM_free = [t_l1, t_var, t_m]
            return ap_toks, [ap_toks[-1]]

        t_psy_free_ln = [t_rbc, t_psy_evac[1]]
        cast_toks = []

        def emit_casts(i2, v2tok):
            for fci in (2 * i2, 2 * i2 + 1):
                sl4 = fci % 4
                tc1 = sca(lambda e, fci=fci, sl4=sl4, x0=x0: e.copy(
                    xb16[:, sl4 * Q:(sl4 + 1) * Q], x0[:, fci * Q:(fci + 1) * Q]),
                    waits=[v2tok, t_xb16_free[sl4]])
                tc2 = vec(lambda e, fci=fci, sl4=sl4: e.tensor_scalar(
                    xq8[:, fci * Q:(fci + 1) * Q], xb16[:, sl4 * Q:(sl4 + 1) * Q],
                    sccol.ap()[:, 0:1], None, ALU.mult), waits=[tc1])
                cast_toks.append(tc2)
                t_xb16_free[sl4] = tc2

        ln1_ap, psy_readers = ln_phase(
            lambda fci: x0[:, fci * Q:(fci + 1) * Q],
            lambda fci: x0[:, fci * Q:(fci + 1) * Q],
            c_eps5, c_i1024, g_evac, extra_apply_w=emit_casts,
            pair_t=(x0, FC * Q, x0, FC * Q))
        t_psy_free_ln = [ln1_ap[-1], ln1_ap[-1]]

        if STOP_PHASE == "H":
            break
        ffn_evac = [None] * FC
        t_t1_free = None
        t1_hist = []
        tv2_hist = []
        for mb in range(FC):
            bank = mb % 4
            last = None
            for kc in range(FC):
                def mmf(e, mb=mb, kc=kc, bank=bank, st=(kc == 0), sp=(kc == FC - 1)):
                    return e.matmul(psBIG[:, bank * 512:bank * 512 + Q],
                                    wf8[:, kc * HID + mb * 128:kc * HID + (mb + 1) * 128],
                                    xq8[:, kc * Q:(kc + 1) * Q], start=st, stop=sp)
                wts = [cast_toks[kc]]
                if kc == 0:
                    wts += [bigfree[bank], t_wf8[0], t_wf8[1]]
                last = ten(mmf, waits=wts)
            tb = xsp[:, mb * Q:(mb + 1) * Q]
            tv1 = sca(lambda e, mb=mb, bank=bank, tb=tb: e.activation(
                tb, psBIG[:, bank * 512:bank * 512 + Q],
                AF.Identity, scale=sccol.ap()[:, 1:2],
                bias=fbc32.ap()[:, mb:mb + 1]),
                waits=[last, t_fb32, free_xsp if mb == 0 else None])
            tv2 = tv1
            bigfree[bank] = tv1
            tv2_hist.append(tv2)
            if mb % 2 == 1:
                i2 = mb // 2
                tv3 = vec(lambda e, i2=i2, x0=x0: e.tensor_tensor(
                    x0[:, i2 * 2 * Q:(i2 + 1) * 2 * Q],
                    x0[:, i2 * 2 * Q:(i2 + 1) * 2 * Q],
                    xsp[:, i2 * 2 * Q:(i2 + 1) * 2 * Q], ALU.add),
                    waits=[tv2_hist[mb - 1], tv2, ln1_ap[mb - 1], ln1_ap[mb],
                           cast_toks[mb - 1], cast_toks[mb]])
                t1_hist.append(tv3)
                ffn_evac[mb - 1] = tv3
                ffn_evac[mb] = tv3

        if STOP_PHASE == "I":
            break
        free_xsp = t1_hist[-1] if t1_hist else free_xsp
        if r + 1 < n_rounds:
            D6_pre[r + 1] = emit_d6block(r + 1)
        ln2_ap, _ = ln_phase(
            lambda fci: x0[:, fci * Q:(fci + 1) * Q],
            lambda fci: ytl[:, fci * Q:(fci + 1) * Q],
            c_eps5, c_i1024, ffn_evac, pair_t=(x0, FC * Q, ytl, PC * Q))

        t_psy_free = [ln2_ap[-1], ln2_ap[-1]]

        mtb_free[1] = [tb_last_ln, ln2_ap[-1]]
        t_out = out_dma(r, waits=ln2_ap + [free_ytl])
        free_xbc = t_out
        free_ytl = t_out
        x0_next_waits = [ln2_ap[-1], P.tok("SS")]

    P.op("sync", None, waits=[P.tok("SO")])
    P.op("gpsimd", None, waits=[P.tok("SO")])

    block = es.enter_context(nc.Block())
    for name in ("sync", "gpsimd", "tensor", "vector", "scalar"):
        def make(nm):
            def body(eng):
                P.emit(nm, eng, sems)
            return body
        getattr(block, name)(make(name))
    es.close()
    return nc


_host_cache = {}


def _prep_host(inputs):
    xi = np.ascontiguousarray(np.asarray(inputs["L0_attn_hidden_states"], np.float32))
    w_in_t = np.asarray(inputs["L0_attn_in_proj_w"], np.float32).T
    w_in = np.zeros((NBLK_IN, 128, FC * 128), np.float32)
    for blk in range(NBLK_IN):
        bs = 32 if blk == NBLK_IN - 1 else 128
        blkw = w_in_t[:, 128 * blk:128 * blk + bs]
        w_in[blk, :, 0:FC * bs] = blkw.reshape(FC, 128, bs).transpose(1, 0, 2).reshape(128, FC * bs)
    w_out_t = np.asarray(inputs["L0_attn_out_proj_w"], np.float32).T
    w_out = np.zeros((FC, 128, PC * 128), np.float32)
    for blk in range(FC):
        blkw = w_out_t[:, 128 * blk:128 * blk + 128]
        w_out[blk] = blkw.reshape(PC, 128, 128).transpose(1, 0, 2).reshape(128, PC * 128)
    w_ffn = np.ascontiguousarray(np.asarray(inputs["L0_ffn_kernel"]))
    cw = np.asarray(inputs["L0_attn_conv_w"], np.float32)
    convw = np.ascontiguousarray(cw.reshape(CC, 128, KCONV).transpose(1, 0, 2).reshape(128, CC * KCONV))
    cb = np.asarray(inputs["L0_attn_conv_b"], np.float32)
    convb = np.ascontiguousarray(cb.reshape(CC, 128).T)
    dtb = np.ascontiguousarray(np.asarray(inputs["L0_attn_dt_bias"], np.float32).reshape(NH, 1))
    alog = np.ascontiguousarray(np.asarray(inputs["L0_attn_A_log"], np.float32).reshape(NH, 1))
    dvec = np.ascontiguousarray(
        np.repeat(np.asarray(inputs["L0_attn_D"], np.float32), PH).reshape(PC, 128).T)
    nwv = np.ascontiguousarray(
        np.asarray(inputs["L0_attn_norm_weight"], np.float32).reshape(PC, 128).T)
    fbias = np.ascontiguousarray(
        np.asarray(inputs["L0_ffn_bias"]).reshape(FC, 128).T)
    sin = np.asarray(inputs["L0_ffn_input_scale"], np.float32).reshape(1, 1)
    skk = np.asarray(inputs["L0_ffn_kernel_scale"], np.float32).reshape(1, 1)
    idn = np.eye(128, dtype=np.float32)
    i_idx = np.arange(Q)
    mka = np.zeros((128, 2 * Q), np.float32)
    for jb in range(2):
        j_glob = jb * 128 + np.arange(128)
        mka[:, jb * Q:(jb + 1) * Q] = np.where(
            i_idx[None, :] >= j_glob[:, None], 0.0, NEG_BIG).astype(np.float32)
    common = dict(win=w_in, wout=w_out, wffn=w_ffn, convw=convw, convb=convb,
                  dtb=dtb, alog=alog, dvec=dvec, nwv=nwv, fbias=fbias,
                  sin=sin, skk=skk, idn=idn, mka=mka)
    return xi, common


def _shard(xi, ci, per=NSEQ):
    a = xi[ci * per:(ci + 1) * per].reshape(per, NCHUNK, Q, FC, 128)
    return np.ascontiguousarray(a.transpose(0, 1, 4, 3, 2).reshape(
        per * NCHUNK, 128, FC * Q))


def _unshard(arr, per=NSEQ):
    a = np.asarray(arr, np.float32).reshape(per, NCHUNK, 128, FC, Q)
    return np.ascontiguousarray(a.transpose(0, 1, 4, 3, 2)).reshape(
        per, SEQLEN, HID)


def kernel(**inputs):
    xi, common = _prep_host(inputs)
    B = xi.shape[0]
    n_cores = 8
    per = B // n_cores
    nc = build_program()
    in_maps = []
    for ci in range(n_cores):
        m = dict(common)
        m["xin"] = _shard(xi, ci, per)
        in_maps.append(m)
    res = run_bass_kernel_spmd(nc, in_maps, core_ids=list(range(n_cores)))
    out = np.concatenate(
        [_unshard(r["yout"], per) for r in res.results], axis=0)
    return out


# version 49
# speedup vs baseline: 1.0547x; 1.0547x over previous
import numpy as np
import ml_dtypes
from contextlib import ExitStack

import concourse.bass as bass
import concourse.mybir as mybir
from concourse.bass_utils import run_bass_kernel_spmd

F32 = mybir.dt.float32
BF16 = mybir.dt.bfloat16
FP8 = mybir.dt.float8e4
ALU = mybir.AluOpType
AF = mybir.ActivationFunctionType

HID = 1024
INTER = 2048
STATE = 128
NH = 32
PH = 64
KCONV = 4
Q = 256
CONV = 2304
DIN = 4384
FC = HID // 128
PC = INTER // 128
CC = CONV // 128
NBLK_IN = 35
SEQLEN = 4096
NSEQ = 2
NCHUNK = SEQLEN // Q
RAWW = Q + 3
NEG_BIG = -1.0e30


class Prog:

    def __init__(self):
        self.ops = []
        self.counts = {}

    def op(self, eng, fn, waits=(), inc=None, inc_k=1):
        wm = {}
        for t in waits:
            if t is None:
                continue
            s, v = t
            if v > wm.get(s, -1):
                wm[s] = v
        w = tuple(wm.items())
        tok = None
        if inc is not None:
            self.counts[inc] = self.counts.get(inc, 0) + inc_k
            tok = (inc, self.counts[inc])
        self.ops.append((eng, fn, w, inc, inc_k))
        return tok

    def tok(self, sem):
        return (sem, self.counts.get(sem, 0))

    def emit(self, name, eng, sems):
        acq = {}
        for ename, fn, waits, inc, inc_k in self.ops:
            if ename != name:
                continue
            for (s, v) in waits:
                if v > acq.get(s, 0):
                    eng.wait_ge(sems[s], v)
                    acq[s] = v
            if fn is None:
                continue
            inst = fn(eng)
            if inc is not None:
                inst.then_inc(sems[inc], inc_k)


STOP_PHASE = None


def build_program(n_rounds=NSEQ * NCHUNK):
    nc = bass.Bass("TRN2", target_bir_lowering=False)

    xin = nc.dram_tensor("xin", [NSEQ * NCHUNK, 128, FC * Q], F32, kind="ExternalInput")
    win = nc.dram_tensor("win", [NBLK_IN, 128, FC * 128], F32, kind="ExternalInput")
    wout = nc.dram_tensor("wout", [FC, 128, PC * 128], F32, kind="ExternalInput")
    wffn = nc.dram_tensor("wffn", [HID, HID], BF16, kind="ExternalInput")
    convw = nc.dram_tensor("convw", [128, CC * KCONV], F32, kind="ExternalInput")
    convb = nc.dram_tensor("convb", [128, CC], F32, kind="ExternalInput")
    dtb = nc.dram_tensor("dtb", [NH, 1], F32, kind="ExternalInput")
    alog = nc.dram_tensor("alog", [NH, 1], F32, kind="ExternalInput")
    dvec = nc.dram_tensor("dvec", [128, PC], F32, kind="ExternalInput")
    nwv = nc.dram_tensor("nwv", [128, PC], F32, kind="ExternalInput")
    fbias = nc.dram_tensor("fbias", [128, FC], BF16, kind="ExternalInput")
    sin_t = nc.dram_tensor("sin", [1, 1], F32, kind="ExternalInput")
    skk_t = nc.dram_tensor("skk", [1, 1], F32, kind="ExternalInput")
    idn_t = nc.dram_tensor("idn", [128, 128], F32, kind="ExternalInput")
    mka_t = nc.dram_tensor("mka", [128, 2 * Q], F32, kind="ExternalInput")
    yout = nc.dram_tensor("yout", [NSEQ * NCHUNK, 128, FC * Q], F32, kind="ExternalOutput")
    win_hl = nc.dram_tensor("win_hl", [NBLK_IN, 128, 2 * FC * 128], BF16)
    wout_hl = nc.dram_tensor("wout_hl", [FC, 128, 2 * PC * 128], BF16)

    es = ExitStack()

    def S(name, shape, dt=F32):
        return es.enter_context(nc.sbuf_tensor(name, shape, dt))

    x0b = [S(f"x0_{j}", [128, FC * Q]) for j in range(3)]
    raw = S("raw", [128, CC * RAWW])
    xbc = S("xbc", [128, CC * Q])
    ytl = S("ytl", [128, PC * Q])
    dtr = S("dtr", [NH, Q])
    dts = S("dts", [NH, Q])
    seg = S("seg", [NH, Q])
    dcy = S("dcy", [NH, Q])
    sp1 = S("sp1", [NH, Q])
    msk2 = S("msk2", [NH, 6 * Q])
    segT = S("segT", [128, 2 * NH])
    dtT = S("dtT", [128, 2 * NH])
    dcyT = S("dcyT", [128, 2 * NH])
    btok = S("btok", [128, 2 * 128])
    cdc = S("cdc", [128, NH])
    cdr = S("cdr", [1, NH])
    xdtT = [S(f"xdtT{j}", [128, INTER]) for j in range(2)]
    xddT = [S(f"xddT{j}", [128, INTER]) for j in range(2)]
    mtb = S("mtb", [128, 12 * Q])
    segB = S("segB", [128, 6 * Q])
    csg = S("csg", [128, 6 * Q])
    prevb = [S(f"prev{j}", [128, INTER]) for j in range(2)]
    winb = [S(f"winb{j}", [128, 2 * FC * 128], BF16) for j in range(3)]
    woutb = [S(f"woutb{j}", [128, 2 * PC * 64], BF16) for j in range(2)]
    wf8 = S("wf8", [128, FC * HID], FP8)
    xb16 = S("xb16", [128, 4 * Q], BF16)
    xsp = S("xsp", [128, 2 * FC * Q], BF16)
    xq8 = S("xq8", [128, FC * Q], FP8)
    scr = S("scr", [128, 2 * Q])
    idn = S("idn_s", [128, 128])
    mka = S("mka_s", [128, 2 * Q])
    onec = S("onec", [128, 1])
    ones32 = S("ones32", [NH, 128])
    oner = ones32.ap()[0:1, :]
    cc1 = S("cc1", [128, 1])
    ccol = S("ccol", [1, 8])
    dtbc = S("dtbc", [NH, 1])
    acol = S("acol", [NH, 1])
    dvc = S("dvc", [128, PC])
    nwc = S("nwc", [128, PC])
    inwc = S("inwc", [128, PC])
    fbc = S("fbc", [128, FC], BF16)
    cwc = S("cwc", [128, CC * KCONV])
    cbc = S("cbc", [128, CC])
    sccol = S("sccol", [128, 2])
    fbc32 = S("fbc32", [128, FC])
    sct = S("sct", [1, 4])

    rows = None
    psBIG = es.enter_context(nc.psum_tensor("psBIG", [128, 2048], F32))
    psCB = es.enter_context(nc.psum_tensor("psCB", [128, 512], F32))
    psSM = es.enter_context(nc.psum_tensor("psSM", [128, 512], F32))
    psY0 = es.enter_context(nc.psum_tensor("psY0", [128, 512], F32))
    psY1 = es.enter_context(nc.psum_tensor("psY1", [128, 512], F32))
    psY = [psY0, psY1]

    sem_names = ["SC", "SF0", "SF1", "SH0", "SH1", "SW0", "SW1", "SW2", "SWO0", "SWO1",
                 "SX0", "SX1", "SX2", "SL0", "SL1", "SO", "ST", "SV", "SS"]
    sems = {n: es.enter_context(nc.semaphore(n)) for n in sem_names}

    rows = bass.AP(mtb, 4 * Q, [[12 * Q, 1], [1, 3 * Q]])

    P = Prog()

    def dma(q, out_ap, in_ap, sem, waits=()):
        return P.op(q, lambda e, o=out_ap, i=in_ap: e.dma_start(out=o, in_=i),
                    waits=waits, inc=sem, inc_k=16)

    def ten(fn, waits=()):
        return P.op("tensor", fn, waits=waits, inc="ST")

    def vec(fn, waits=()):
        return P.op("vector", fn, waits=waits, inc="SV")

    def sca(fn, waits=()):
        return P.op("scalar", fn, waits=waits, inc="SS")

    c_eps5 = ccol.ap()[0:1, 0:1]
    c_eps6 = ccol.ap()[0:1, 1:2]
    c_i1024 = ccol.ap()[0:1, 2:3]
    c_i2048 = ccol.ap()[0:1, 3:4]
    c_nhalf = ccol.ap()[0:1, 4:5]

    dma("sync", idn[:], idn_t.ap()[:], "SC")
    dma("sync", mka[:], mka_t.ap()[:], "SC")
    dma("sync", cwc[:], convw.ap()[:], "SC")
    dma("sync", cbc[:], convb.ap()[:], "SC")
    dma("sync", dtbc[:], dtb.ap()[:], "SC")
    dma("sync", sp1[0:NH, 0:1], alog.ap()[:], "SC")
    dma("sync", dvc[:], dvec.ap()[:], "SC")
    dma("sync", nwc[:], nwv.ap()[:], "SC")
    dma("sync", fbc[:], fbias.ap()[:], "SC")
    dma("sync", sct[0:1, 0:1], sin_t.ap()[:], "SC")
    t_cst = dma("sync", sct[0:1, 1:2], skk_t.ap()[:], "SC")
    t_idn = t_mka = t_cw = t_cb2 = t_dtb = t_alog = t_cst
    t_dv = t_nw = t_fb = t_sin = t_skk = t_cst

    vec(lambda e: e.memset(onec[:], 1.0))
    vec(lambda e: e.memset(ones32[:], 1.0))
    vec(lambda e: e.memset(cc1[:], 1.0))
    vec(lambda e: e.memset(ccol[0:1, 0:1], 1e-5))
    vec(lambda e: e.memset(ccol[0:1, 1:2], 1e-6))
    vec(lambda e: e.memset(ccol[0:1, 2:3], 1.0 / HID))
    vec(lambda e: e.memset(ccol[0:1, 3:4], 1.0 / INTER))
    vec(lambda e: e.memset(ccol[0:1, 4:5], -0.5))
    t_pv0 = vec(lambda e: e.memset(prevb[0][:], 0.0))
    t_ms = vec(lambda e: e.memset(raw[:], 0.0))

    t_a1 = sca(lambda e: e.activation(acol[:], sp1[0:NH, 0:1], AF.Exp), waits=[t_alog])
    t_fb32 = sca(lambda e: e.copy(fbc32[:], fbc[:]), waits=[t_fb])
    t_a2 = vec(lambda e: e.tensor_scalar(acol[:], acol[:], -1.0, None, ALU.mult),
               waits=[t_a1])
    t_inw = vec(lambda e: e.reciprocal(inwc[:], nwc[:]), waits=[t_cst])
    t_r1 = vec(lambda e: e.reciprocal(sct[0:1, 2:3], sct[0:1, 0:1]), waits=[t_sin])
    t_r2 = vec(lambda e: e.tensor_tensor(sct[0:1, 3:4], sct[0:1, 0:1], sct[0:1, 1:2],
                                         ALU.mult), waits=[t_skk])
    t_bc0 = ten(lambda e: e.matmul(psSM[:, 0:2], oner, sct[0:1, 2:4],
                                   start=True, stop=True), waits=[t_r1, t_r2])
    t_scc = vec(lambda e: e.tensor_copy(sccol[:], psSM[:, 0:2]), waits=[t_bc0])

    tq_prev = [None, None]
    for kc in range(FC):
        half = kc % 2
        dst = xsp[:, half * HID:(half + 1) * HID]
        in_ap = bass.AP(wffn, kc * 128 * HID, [[HID, 128], [1, HID]])
        tl = dma("sync", dst, in_ap, f"SF{half}", waits=[tq_prev[half]])
        tq_prev[half] = vec(
            lambda e, kc=kc, dst=dst: e.tensor_scalar(
                wf8[:, kc * HID:(kc + 1) * HID], dst, sccol.ap()[:, 0:1], None,
                ALU.mult),
            waits=[tl, t_scc])
    t_wf8 = tq_prev[0], tq_prev[1]

    t_wprep_sp = [tq_prev[0], tq_prev[1]]
    t_wprep_hist = [[tq_prev[0]], [tq_prev[1]]]
    t_wprep_st = None
    for blk in range(NBLK_IN + FC):
        wb = blk % 2
        if blk < NBLK_IN:
            in_ap = bass.AP(win, blk * 128 * FC * 128, [[FC * 128, 128], [1, FC * 128]])
        else:
            in_ap = bass.AP(wout, (blk - NBLK_IN) * 128 * PC * 128,
                            [[PC * 128, 128], [1, 2 * PC * 64]])
        w = FC * 128 if blk < NBLK_IN else 2 * PC * 64
        stg = ytl[:, wb * 2048:wb * 2048 + w]
        xo_ = 0 if (w == 2048 or wb == 0) else 1024
        tl = dma("sync", stg, in_ap, f"SWO{wb}", waits=[t_wprep_sp[wb]])
        th = sca(lambda e, stg=stg, w=w, xo_=xo_: e.copy(
            xsp[:, xo_:xo_ + w], stg),
            waits=[tl, tq_prev[0], tq_prev[1]] + list(t_wprep_hist[wb]) +
                  (list(t_wprep_hist[1 - wb]) if w == 2048 else []))
        tl2 = vec(lambda e, stg=stg, w=w, xo_=xo_: e.tensor_tensor(
            xsp[:, 2048 + xo_:2048 + xo_ + w], stg, xsp[:, xo_:xo_ + w],
            ALU.subtract), waits=[th])
        t_wprep_sp[wb] = tl2
        if blk < NBLK_IN:
            oh = bass.AP(win_hl, blk * 128 * 2 * FC * 128,
                         [[2 * FC * 128, 128], [1, FC * 128]])
            ol = bass.AP(win_hl, blk * 128 * 2 * FC * 128 + FC * 128,
                         [[2 * FC * 128, 128], [1, FC * 128]])
        else:
            b2 = blk - NBLK_IN
            oh = bass.AP(wout_hl, b2 * 128 * 2 * PC * 128,
                         [[2 * PC * 128, 128], [1, 2 * PC * 64]])
            ol = bass.AP(wout_hl, b2 * 128 * 2 * PC * 128 + PC * 128,
                         [[2 * PC * 128, 128], [1, 2 * PC * 64]])
        t_oh = dma("scalar", oh, xsp[:, xo_:xo_ + w], f"SH{wb}", waits=[tl2])
        t_wprep_st = dma("gpsimd", ol, xsp[:, 2048 + xo_:2048 + xo_ + w],
                         f"SL{wb}", waits=[tl2])
        t_wprep_hist[wb] = [t_oh, t_wprep_st]
    t_wprep_done = [P.tok("SH0"), P.tok("SH1"), P.tok("SL0"), P.tok("SL1")]
    free_xsp = t_wprep_st

    free_winb = [None, None, None]
    free_woutb = [None, None]
    bigfree = [None, None, None, None]
    t_halo = None
    free_xbc = None
    free_ytl = None
    free_x0 = None
    t_psy_free = [t_scc, t_scc]
    t_psSM_free = [t_scc]
    psCB_free = None
    scr_free = [None, None]
    t_xb16_free = [None, None, None, None]
    t_out = None

    def win_dma(buf, blk, waits):
        in_ap = bass.AP(win_hl, blk * 128 * 2 * FC * 128,
                        [[2 * FC * 128, 128], [1, 2 * FC * 128]])
        return dma("sync", winb[buf][:, :], in_ap,
                   f"SW{buf}", waits=list(waits) + t_wprep_done)

    def wout_dma(buf, blk, half, waits):
        in_ap = bass.AP(wout_hl, blk * 128 * 2 * PC * 128 + half * PC * 64,
                        [[2 * PC * 128, 128], [PC * 128, 2], [1, PC * 64]])
        return dma("sync", woutb[buf][:, :], in_ap, f"SWO{buf}",
                   waits=list(waits) + t_wprep_done)

    def x0_dma(r, waits):
        in_ap = bass.AP(xin, r * 128 * FC * Q, [[FC * Q, 128], [1, FC * Q]])
        return dma("gpsimd", x0b[r % 3][:], in_ap, f"SX{r % 3}", waits=waits)

    def out_dma(r, waits):
        out_ap = bass.AP(yout, r * 128 * FC * Q, [[FC * Q, 128], [1, FC * Q]])
        return dma("gpsimd", out_ap, ytl[:, 0:FC * Q], "SO", waits=waits)

    t_x0 = x0_dma(0, [])

    x0_next_waits = []
    x0_tok = {0: t_x0}
    if NSEQ * NCHUNK > 1:
        x0_tok[1] = x0_dma(1, [])
    halo_tok = {0: t_ms}
    B_state = {}
    dtr_free = None
    prev_writer = [t_pv0, None]
    prev_reader_last = [None, None]

    R_chain = {}
    segfam_free = []
    NPAIR = NH // 2
    LAG = 2
    DEP = LAG + 1
    segB_free = [None] * DEP
    mtb_free = [[], [], []]
    csg_free = [None] * DEP
    t_msk_free = [None] * DEP

    R_conv = {}
    R_silu = {}
    D6_pre = {}

    def emit_silus_for(r):
        tap_toks = R_conv[r]
        toks = [None] * CC
        for cci in [0, 1, 2, 3, 16, 17] + list(range(4, 16)):
            toks[cci] = sca(lambda e, cci=cci: e.activation(
                xbc[:, cci * Q:(cci + 1) * Q], xbc[:, cci * Q:(cci + 1) * Q],
                AF.Silu, bias=cbc.ap()[:, cci:cci + 1]),
                waits=[tap_toks[cci], t_cst])
        R_silu[r] = toks

    def conv_generator(r, ypart):
        evac_ = B_state[r]['evac']
        conv_toks = [None] * CC
        R_conv[r] = conv_toks
        for cci in [16, 17] + list(range(16)):
            base = cci * RAWW
            assert evac_[cci] is not None
            w0 = [evac_[cci], halo_tok[r], t_cst, P.tok("ST")]
            if cci < 16:
                w0.append(ypart[cci])
            else:
                w0.append(P.tok("SV"))
            tt = vec(lambda e, cci=cci, base=base: e.tensor_scalar(
                xbc[:, cci * Q:(cci + 1) * Q], raw[:, base:base + Q],
                cwc.ap()[:, cci * KCONV:cci * KCONV + 1], None, ALU.mult),
                waits=w0)
            for k in range(1, KCONV):
                tt = vec(lambda e, cci=cci, base=base, k=k: e.scalar_tensor_tensor(
                    xbc[:, cci * Q:(cci + 1) * Q], raw[:, base + k:base + k + Q],
                    cwc.ap()[:, cci * KCONV + k:cci * KCONV + k + 1],
                    xbc[:, cci * Q:(cci + 1) * Q], ALU.mult, ALU.add), waits=[tt])
            conv_toks[cci] = tt
            yield
        halo_ap = bass.AP(raw, 0, [[CC * RAWW, 128], [RAWW, CC], [1, 3]])
        if (r + 1) % NCHUNK == 0:
            halo_tok[r + 1] = vec(lambda e, a=halo_ap: e.memset(a, 0.0),
                                  waits=[P.tok("SV")])
        else:
            src = bass.AP(raw, Q, [[CC * RAWW, 128], [RAWW, CC], [1, 3]])
            halo_tok[r + 1] = vec(lambda e, d=halo_ap, s=src: e.tensor_copy(d, s),
                                  waits=[P.tok("SV")])


    def emit_dt_chain(r):
        nonlocal dtr_free
        t_dtr_ = B_state[r]['t_dtr']
        w0 = list(segfam_free)
        t_d = sca(lambda e: e.activation(sp1[0:NH, :], dtr[:], AF.Abs,
                                         bias=dtbc.ap()[:]),
                  waits=[t_dtr_, t_cst] + w0)
        t_d2 = sca(lambda e: e.activation(sp1[0:NH, :], sp1[0:NH, :], AF.Exp,
                                          scale=-1.0), waits=[t_d])
        t_sp = sca(lambda e: e.activation(sp1[0:NH, :], sp1[0:NH, :], AF.Ln,
                                          bias=cc1.ap()[0:NH, 0:1]), waits=[t_d2])
        t_rl = sca(lambda e: e.activation(dts[:], dtr[:], AF.Relu,
                                          bias=dtbc.ap()[:]), waits=w0)
        dtr_free = t_rl
        t_dt = vec(lambda e: e.tensor_tensor(dts[:], dts[:], sp1[0:NH, :],
                                             ALU.add), waits=[t_sp, t_rl] + w0)
        t_dta = vec(lambda e: e.tensor_scalar(sp1[0:NH, :], dts[:], acol.ap()[:],
                                              None, ALU.mult), waits=[t_dt, t_a2])
        t_seg = vec(lambda e: e.tensor_tensor_scan(seg[:], sp1[0:NH, :],
                                                   sp1[0:NH, :], 0.0,
                                                   ALU.add, ALU.bypass),
                    waits=[t_dta])
        t_dcy = sca(lambda e: e.activation(dcy[:], seg[:], AF.Exp,
                                           bias=seg.ap()[:, Q - 1:Q], scale=-1.0),
                    waits=[t_seg])
        cbfree = psCB_free
        tr = []
        for si, (srcT, stok, dstT) in enumerate(
                [(seg, t_seg, segT), (dts, t_dt, dtT), (dcy, t_dcy, dcyT)]):
            for jb in range(2):
                col = (si * 2 + jb) * 32
                tk = ten(lambda e, srcT=srcT, jb=jb, col=col: e.transpose(
                    psCB[:, col:col + NH], srcT[:, jb * 128:(jb + 1) * 128],
                    idn[0:NH, 0:NH]),
                    waits=[stok, t_cst, cbfree])
                tr.append((tk, dstT, col, jb))
        t_cd1 = ten(lambda e: e.transpose(psCB[0:1, 192:192 + NH], seg[:, Q - 1:Q],
                                          idn[0:NH, 0:NH]), waits=[t_seg])
        ev_tr = []
        for tk, dstT, col, jb in tr:
            ev_tr.append(vec(lambda e, dstT=dstT, col=col, jb=jb: e.tensor_copy(
                dstT[:, jb * NH:(jb + 1) * NH], psCB[:, col:col + NH]),
                waits=[tk, t_cd1]))
        t_cd2 = sca(lambda e: e.activation(cdr[:], psCB[0:1, 192:192 + NH], AF.Exp),
                    waits=[t_cd1])
        t_cd3 = ten(lambda e: e.matmul(psCB[:, 224:224 + NH], oner, cdr[:],
                                       start=True, stop=True),
                    waits=[t_cd2] + ev_tr)
        t_cd = sca(lambda e: e.copy(cdc[:], psCB[:, 224:224 + NH]),
                   waits=[t_cd3])
        R_chain[r] = dict(t_dt=t_dt, t_seg=t_seg, t_dcy=t_dcy,
                          t_segT=ev_tr[1], t_dtT=ev_tr[3], t_dcyT=ev_tr[5],
                          t_cd=t_cd, t_cd3=t_cd3, ev_last=ev_tr[-1],
                          tr_last=tr[-1][0])

    def emit_btok(r):
        ch = R_chain[r]
        t_bt = []
        for jb in range(2):
            tk = ten(lambda e, jb=jb: e.transpose(
                psCB[:, 256 + jb * 128:256 + (jb + 1) * 128],
                xbc[:, 16 * Q + jb * 128:16 * Q + jb * 128 + 128], idn[:]),
                waits=[R_silu[r][16], ch['t_cd3'], ch['t_cd'], ch['ev_last']])
            t_bt.append(sca(lambda e, jb=jb: e.copy(
                btok[:, jb * 128:(jb + 1) * 128],
                psCB[:, 256 + jb * 128:256 + (jb + 1) * 128]), waits=[tk]))
        ch['bt'] = t_bt

    def B_prologue(r):
        x0r = x0b[r % 3]
        blocks = [NBLK_IN - 1] + list(range(16, NBLK_IN - 1))
        w_tok = {}
        w_tok[blocks[0]] = win_dma(0, blocks[0], [free_winb[0]])
        w_tok[blocks[1]] = win_dma(1, blocks[1], [free_winb[1]])
        w_tok[blocks[2]] = win_dma(2, blocks[2], [free_winb[2]])
        th = sca(lambda e, x0=x0r: e.copy(xsp[:, 0:2048], x0[:]),
                 waits=[x0_tok[r], free_xsp] + t_wprep_done)
        t_xsp = vec(lambda e, x0=x0r: e.tensor_tensor(
            xsp[:, 2048:4096], x0[:], xsp[:, 0:2048], ALU.subtract),
            waits=[th, free_xsp])
        return blocks, w_tok, t_xsp

    def B_generator(r, blocks, w_tok, t_xsp):
        nonlocal dtr_free, free_xsp
        evac = [None] * CC
        st_ = {'evac': evac}
        B_state[r] = st_
        for bi, blk in enumerate(blocks):
            bs = 32 if blk == NBLK_IN - 1 else 128
            bank = bi % 4
            buf = bi % 3
            mm_w = [w_tok[blk], t_xsp, bigfree[bank]]
            last = None
            for kc in range(FC):
                for ph, (wo, xo) in enumerate(((0, 0), (0, 2048), (1024, 0))):
                    def mmf(e, buf=buf, kc=kc, bs=bs, bank=bank, wo=wo, xo=xo,
                            st=(kc == 0 and ph == 0),
                            sp=(kc == FC - 1 and ph == 2)):
                        return e.matmul(psBIG[0:bs, bank * 512:bank * 512 + Q],
                                        winb[buf][:, wo + kc * bs:wo + (kc + 1) * bs],
                                        xsp[:, xo + kc * Q:xo + (kc + 1) * Q],
                                        start=st, stop=sp)
                    last = ten(mmf, waits=mm_w if (kc == 0 and ph == 0) else ())
            if bi == len(blocks) - 1:
                free_xsp = last
            free_winb[buf] = last
            if bi + 3 < len(blocks):
                w_tok[blocks[bi + 3]] = win_dma(buf, blocks[bi + 3], [last])
            if blk < NBLK_IN - 1:
                cci = blk - 16
                tv = sca(lambda e, cci=cci, bank=bank: e.copy(
                    raw[:, cci * RAWW + 3:cci * RAWW + 3 + Q],
                    psBIG[:, bank * 512:bank * 512 + Q]),
                    waits=[last, halo_tok[r]])
                evac[cci] = tv
            else:
                tv = sca(lambda e, bank=bank: e.copy(
                    dtr[:], psBIG[0:NH, bank * 512:bank * 512 + Q]),
                    waits=[last, dtr_free])
                st_['t_dtr'] = tv
            bigfree[bank] = tv
            yield

    for r in range(n_rounds):
        c = r % NCHUNK
        seq_start = (c == 0)
        x0 = x0b[r % 3]
        if r + 2 < n_rounds:
            x0_tok[r + 2] = x0_dma(r + 2, waits=list(x0_next_waits))

        if r == 0:
            for _ in B_generator(0, *B_prologue(0)):
                pass
            emit_dt_chain(0)
            for _ in conv_generator(0, [None] * PC):
                pass
            emit_silus_for(0)
            emit_btok(0)
        if r + 1 < n_rounds:
            bgen = B_generator(r + 1, *B_prologue(r + 1))
        else:
            bgen = None
        evac = B_state[r]['evac']
        t_dtr = B_state[r]['t_dtr']
        prev_cur = prevb[r % 2]
        prev_new = prevb[(r + 1) % 2]
        if seq_start and r > 0:
            t_prev_ok = vec(lambda e, pv=prev_cur: e.memset(pv[:], 0.0),
                            waits=[prev_writer[r % 2], prev_reader_last[r % 2]])
        else:
            t_prev_ok = prev_writer[r % 2]

        if STOP_PHASE == "B":
            break
        ch = R_chain[r]
        t_dt, t_seg, t_dcy = ch["t_dt"], ch["t_seg"], ch["t_dcy"]
        t_segT, t_dtT, t_dcyT = ch["t_segT"], ch["t_dtT"], ch["t_dcyT"]
        t_cd = ch["t_cd"]
        t_bt = ch["bt"]

        if STOP_PHASE == "D1":
            break
        conv_toks = R_silu[r]

        def emit_d6block(rr):
            chb = R_chain[rr]
            tcd_b, tbt_b = chb["t_cd"], chb["bt"]
            tdtT_b, tdcyT_b = chb["t_dtT"], chb["t_dcyT"]
            convt = R_silu[rr]
            grp_done = [[None] * 4 for _ in range(2)]
            tr0 = [None] * 4
            for g in range(4):
                trs = None
                for fi in range(4):
                    fc_i = g * 4 + fi
                    wts = [convt[fc_i], t_idn, bigfree[g]]
                    def trf(e, fc_i=fc_i):
                        return e.transpose(
                            psBIG[:, fc_i * 128:(fc_i + 1) * 128],
                            xbc[:, fc_i * Q + 0 * 128: fc_i * Q + 0 * 128 + 128],
                            idn[:])
                    trs = ten(trf, waits=wts)
                tr0[g] = trs
            t_cbm = []
            for jb in range(2):
                t_cbm.append(ten(lambda e, jb=jb: e.matmul(
                    psCB[:, jb * Q:(jb + 1) * Q],
                    xbc[:, 16 * Q + jb * 128:16 * Q + jb * 128 + 128],
                    xbc[:, 17 * Q:18 * Q], start=True, stop=True),
                    waits=[convt[16], convt[17], psCB_free,
                           tbt_b[0], tbt_b[1], tcd_b]))
            for jb in range(2):
                for g in range(4):
                    if jb == 0:
                        trs = tr0[g]
                    else:
                        trs = None
                        for fi in range(4):
                            fc_i = g * 4 + fi
                            def trf(e, fc_i=fc_i, jb=jb):
                                return e.transpose(
                                    psBIG[:, fc_i * 128:(fc_i + 1) * 128],
                                    xbc[:, fc_i * Q + jb * 128: fc_i * Q + jb * 128 + 128],
                                    idn[:])
                            trs = ten(trf, waits=[grp_done[0][g]] if fi == 0 else ())
                    big3 = bass.AP(psBIG, g * 512, [[2048, 128], [PH, 8], [1, PH]])
                    xdt3 = bass.AP(xdtT[jb], g * 512, [[INTER, 128], [PH, 8], [1, PH]])
                    dtb3 = dtT.ap()[:, jb * NH + g * 8:jb * NH + g * 8 + 8].broadcast_to(
                        [128, 8, PH])
                    tx = vec(lambda e, a=xdt3, b=big3, c=dtb3: e.tensor_tensor(
                        a, b, c, ALU.mult), waits=[trs, tdtT_b, grp_done[1][g] if jb == 0 else None])
                    xdd3 = bass.AP(xddT[jb], g * 512, [[INTER, 128], [PH, 8], [1, PH]])
                    dcb3 = dcyT.ap()[:, jb * NH + g * 8:jb * NH + g * 8 + 8].broadcast_to(
                        [128, 8, PH])
                    td = vec(lambda e, a=xdd3, b=xdt3, c=dcb3: e.tensor_tensor(
                        a, b, c, ALU.mult), waits=[tdcyT_b, tx])
                    grp_done[jb][g] = td
            t_xdt_done = P.tok("SV")
            t_st = [None] * 4
            for g in range(4):
                last = None
                for jb in range(2):
                    def stf(e, g=g, jb=jb):
                        return e.matmul(psBIG[:, g * 512:(g + 1) * 512],
                                        btok[:, jb * 128:(jb + 1) * 128],
                                        xddT[jb][:, g * 512:(g + 1) * 512],
                                        start=(jb == 0), stop=(jb == 1))
                    last = ten(stf, waits=[tbt_b[jb], grp_done[0][g],
                                           grp_done[1][g]])
                t_st[g] = last
            return t_cbm, t_xdt_done, t_st

        if r in D6_pre:
            t_cbm, t_xdt_done, t_st = D6_pre.pop(r)
        else:
            t_cbm, t_xdt_done, t_st = emit_d6block(r)


        t_scan = None
        scan_bank_done = [None] * 4

        def emit_scan(h):
            nonlocal t_scan
            g = h // 8
            wts = [t_st[g], t_cd]
            if h == 0:
                wts += [prev_reader_last[(r + 1) % 2], t_prev_ok]
            def scf(e, h=h, pc_=prev_cur, pn_=prev_new):
                return e.scalar_tensor_tensor(
                    pn_[:, h * PH:(h + 1) * PH], pc_[:, h * PH:(h + 1) * PH],
                    cdc.ap()[:, h:h + 1], psBIG[:, h * PH:(h + 1) * PH],
                    ALU.mult, ALU.add)
            t_scan = vec(scf, waits=wts)
            if h % 8 == 7:
                scan_bank_done[g] = t_scan
                bigfree[g] = t_scan

        t_y_parts = [None] * PC
        cgen = conv_generator(r + 1, t_y_parts) if r + 1 < n_rounds else None
        t_psy_evac = list(t_psy_free)
        t_seg_copy = [None] * (NPAIR + 1)
        t_msk_tok = [None] * (NPAIR + 1)
        pair_state = {}

        def emit_msk(p):
            if p >= NPAIR:
                return
            mb_ = p % DEP
            tm0 = sca(lambda e, p=p, mb_=mb_: e.activation(
                msk2[:, mb_ * 2 * Q:mb_ * 2 * Q + Q], seg[:],
                AF.Copy, scale=idn.ap()[0:NH, 2 * p:2 * p + 1]),
                waits=[t_seg, t_cst, t_msk_free[mb_]])
            tm1 = sca(lambda e, p=p, mb_=mb_: e.activation(
                msk2[:, mb_ * 2 * Q + Q:mb_ * 2 * Q + 2 * Q], seg[:],
                AF.Copy, scale=idn.ap()[0:NH, 2 * p + 1:2 * p + 2]),
                waits=[tm0])
            t_msk_tok[p] = tm1

        for p in range(LAG):
            emit_msk(p)

        for p in range(NPAIR + LAG):
            buf = p % DEP
            if p < NPAIR:
                emit_msk(p + LAG)
                tsg = ten(lambda e, buf=buf: e.matmul(
                    psSM[:, 0:2 * Q], ones32[:],
                    msk2[:, buf * 2 * Q:(buf + 1) * 2 * Q],
                    start=True, stop=True),
                          waits=([t_msk_tok[p]] +
                                 ([t_seg_copy[p - 1]] if p > 0 else t_psSM_free)))
                t_msk_free[buf] = tsg
                tcp = sca(lambda e, buf=buf: e.copy(
                    segB[:, buf * 2 * Q:(buf + 1) * 2 * Q], psSM[:, 0:2 * Q]),
                    waits=[tsg, segB_free[buf]])
                t_seg_copy[p] = tcp
                tdfs = []
                tmts2 = []
                for hh in range(2):
                    h = 2 * p + hh
                    base2 = buf * 4 * Q + hh * 2 * Q
                    for jb in range(2):
                        sl = base2 + jb * Q
                        sb = buf * 2 * Q + hh * Q
                        tdf = vec(lambda e, sl=sl, sb=sb, jb=jb, h=h:
                                  e.scalar_tensor_tensor(
                                      mtb[:, sl:sl + Q], segB[:, sb:sb + Q],
                                      segT.ap()[:, jb * NH + h:jb * NH + h + 1],
                                      mka[:, jb * Q:(jb + 1) * Q],
                                      ALU.subtract, ALU.add),
                                  waits=[tcp, t_segT] + list(mtb_free[buf]))
                        tdfs.append(tdf)
                    tex = sca(lambda e, base2=base2: e.activation(
                        mtb[:, base2:base2 + 2 * Q], mtb[:, base2:base2 + 2 * Q],
                        AF.Exp), waits=[tdfs[-2], tdfs[-1]])
                    tmt = vec(lambda e, base2=base2: e.tensor_tensor(
                        mtb[:, base2:base2 + 2 * Q], mtb[:, base2:base2 + 2 * Q],
                        psCB[:, 0:2 * Q], ALU.mult),
                        waits=[tex, t_cbm[0], t_cbm[1]])
                    tmts2.append(tmt)
                tmts = [tmts2[0], tmts2[0], tmts2[1], tmts2[1]]
                tse = sca(lambda e, buf=buf: e.activation(
                    segB[:, buf * 2 * Q:(buf + 1) * 2 * Q],
                    segB[:, buf * 2 * Q:(buf + 1) * 2 * Q], AF.Exp),
                    waits=tdfs)
                tcss = []
                for hh in range(2):
                    sb = buf * 2 * Q + hh * Q
                    tcs = vec(lambda e, buf=buf, hh=hh, sb=sb: e.tensor_tensor(
                        csg[:, sb:sb + Q], xbc[:, 17 * Q:18 * Q],
                        segB[:, sb:sb + Q], ALU.mult),
                        waits=[tse, conv_toks[17], csg_free[buf]])
                    tcss.append(tcs)
                segB_free[buf] = tcss[-1]
                pair_state[p] = (tmts, tcss)
            if p < 4:
                for hh8 in range(8):
                    emit_scan(p * 8 + hh8)
                if p == 3:
                    prev_writer[(r + 1) % 2] = t_scan
            pd = p - LAG
            if 0 <= pd < NPAIR:
                dbuf = pd % DEP
                tmts, tcss = pair_state.pop(pd)
                ps = psY[pd % 2]
                lastmm = None
                for hh in range(2):
                    h = 2 * pd + hh
                    tp = (0, PH * hh)
                    mm_w = [tmts[hh * 2], tmts[hh * 2 + 1]]
                    if hh == 0:
                        mm_w += [t_psy_evac[pd % 2], t_xdt_done]
                    for jb in range(2):
                        sl = dbuf * 4 * Q + (hh * 2 + jb) * Q
                        def ydf(e, h=h, jb=jb, ps=ps, tp=tp, hh=hh, sl=sl,
                                st=(jb == 0)):
                            return e.matmul(
                                ps[PH * hh:PH * hh + PH, 0:Q],
                                xdtT[jb][:, h * PH:(h + 1) * PH],
                                mtb[:, sl:sl + Q],
                                start=st, stop=False, tile_position=tp)
                        lastmm = ten(ydf, waits=mm_w if jb == 0 else ())
                    sb = dbuf * 2 * Q + hh * Q
                    def yof(e, h=h, ps=ps, tp=tp, hh=hh, sb=sb, pc_=prev_cur):
                        return e.matmul(ps[PH * hh:PH * hh + PH, 0:Q],
                                        pc_[:, h * PH:(h + 1) * PH],
                                        csg[:, sb:sb + Q],
                                        start=False, stop=True, tile_position=tp)
                    lastmm = ten(yof, waits=[tcss[hh], t_prev_ok])
                    prev_reader_last[r % 2] = lastmm
                mtb_free[dbuf] = [lastmm]
                csg_free[dbuf] = lastmm
                def evf(e, pc_i=pd, ps=ps):
                    return e.scalar_tensor_tensor(
                        ytl[:, pc_i * Q:(pc_i + 1) * Q],
                        xbc[:, pc_i * Q:(pc_i + 1) * Q],
                        dvc.ap()[:, pc_i:pc_i + 1], ps[:, 0:Q], ALU.mult, ALU.add)
                tev = vec(evf, waits=[lastmm, conv_toks[pd], free_ytl, t_cst])
                t_psy_evac[pd % 2] = tev
                t_y_parts[pd] = tev
            if bgen is not None and p >= 1:
                next(bgen, None)
        segfam_free = [P.tok("SV"), P.tok("SS"), P.tok("ST")]
        psCB_free = P.tok("SV")
        if bgen is not None:
            for _ in bgen:
                pass
        if r + 1 < n_rounds:
            emit_dt_chain(r + 1)
        t_psSM_free = [t_seg_copy[NPAIR - 1]]

        if STOP_PHASE == "D":
            break
        t_rms_mm = None
        th_e_a = sca(lambda e, x0=x0: e.copy(xsp[:, 0:1024], x0[:, 0:1024]),
                     waits=[free_xsp])
        t_xe_a = vec(lambda e, x0=x0: e.tensor_tensor(
            xsp[:, 2048:3072], x0[:, 0:1024], xsp[:, 0:1024], ALU.subtract),
            waits=[th_e_a, free_xsp])
        th_e_b = sca(lambda e, x0=x0: e.copy(xsp[:, 1024:2048], x0[:, 1024:2048]),
                     waits=[th_e_a])
        t_xsp_e = vec(lambda e, x0=x0: e.tensor_tensor(
            xsp[:, 3072:4096], x0[:, 1024:2048], xsp[:, 1024:2048],
            ALU.subtract), waits=[th_e_b, t_xe_a])
        for mb in range(PC):
            bank = mb % 4
            buf = mb % 3
            tw = win_dma(buf, mb, [free_winb[buf]])
            last = None
            for kc in range(FC):
                for ph, (wo, xo) in enumerate(((0, 0), (0, 2048), (1024, 0))):
                    def mmf(e, mb=mb, kc=kc, bank=bank, buf=buf, wo=wo, xo=xo,
                            st=(kc == 0 and ph == 0),
                            sp=(kc == FC - 1 and ph == 2)):
                        return e.matmul(psBIG[:, bank * 512:bank * 512 + Q],
                                        winb[buf][:, wo + kc * 128:wo + (kc + 1) * 128],
                                        xsp[:, xo + kc * Q:xo + (kc + 1) * Q],
                                        start=st, stop=sp)
                    if kc == 0 and ph == 0:
                        e_wts = [tw, bigfree[bank], t_xe_a]
                    elif mb == 0 and kc == 4 and ph == 0:
                        e_wts = [t_xsp_e]
                    else:
                        e_wts = ()
                    last = ten(mmf, waits=e_wts)
            free_winb[buf] = last
            free_x0 = last
            if mb == PC - 1:
                free_xsp = last
            par = mb % 2
            def emit_fstat(mbq):
                nonlocal t_rms_mm
                parq = mbq % 2
                t_sq_f = sca(lambda e, mbq=mbq, parq=parq: e.activation(
                    scr[:, parq * Q:(parq + 1) * Q], ytl[:, mbq * Q:(mbq + 1) * Q],
                    AF.Square, scale=inwc.ap()[:, mbq:mbq + 1]),
                    waits=[t_y_parts[mbq], scr_free[parq], t_inw])
                def smf_f(e, parq=parq, st=(mbq == 0), sp=(mbq == PC - 1)):
                    return e.matmul(psSM[0:1, 0:Q], onec[:],
                                    scr[:, parq * Q:(parq + 1) * Q],
                                    start=st, stop=sp)
                wts_f = [t_sq_f]
                if mbq == 0:
                    wts_f += t_psSM_free
                t_rms_mm = ten(smf_f, waits=wts_f)
                scr_free[parq] = t_rms_mm
            if mb >= 2:
                emit_fstat(mb - 2)
            tsl = sca(lambda e, bank=bank, par=par: e.activation(
                scr[:, par * Q:(par + 1) * Q],
                psBIG[:, bank * 512:bank * 512 + Q], AF.Silu),
                waits=[last, scr_free[par]])
            bigfree[bank] = tsl
            tmu = vec(lambda e, mb=mb, par=par: e.scalar_tensor_tensor(
                ytl[:, mb * Q:(mb + 1) * Q], ytl[:, mb * Q:(mb + 1) * Q],
                nwc.ap()[:, mb:mb + 1],
                scr[:, par * Q:(par + 1) * Q], ALU.mult, ALU.mult),
                waits=[tsl, t_y_parts[mb], t_nw])
            t_y_parts[mb] = tmu
            if mb == PC - 2:
                th_g0 = vec(lambda e: e.tensor_copy(
                    winb[1][:, 0:2048], ytl[:, 0:2048]),
                    waits=[free_winb[1], max(t_y_parts[0:8])])
            if mb == PC - 1:
                t_ysp0 = vec(lambda e: e.tensor_tensor(
                    winb[2][:, 0:2048], ytl[:, 0:2048], winb[1][:, 0:2048],
                    ALU.subtract), waits=[th_g0, free_winb[2]])
            if cgen is not None:
                next(cgen, None)
                if mb >= PC - 2:
                    next(cgen, None)
        if cgen is not None:
            for _ in cgen:
                pass
        emit_fstat(PC - 2)
        emit_fstat(PC - 1)

        if STOP_PHASE == "E":
            break
        t_warm_f = sca(lambda e: e.activation(scr[:, 0:1], scr[:, 0:1], AF.Exp),
                       waits=[scr_free[0]])
        scr_free[0] = t_warm_f
        t_f1 = sca(lambda e: e.activation(rows[:, Q:2 * Q], psSM[0:1, 0:Q], AF.Ln,
                                          scale=c_i2048, bias=c_eps6),
                   waits=[t_rms_mm])
        t_f2 = sca(lambda e: e.activation(rows[:, 0:Q], rows[:, Q:2 * Q], AF.Exp,
                                          scale=c_nhalf), waits=[t_f1])
        t_fb2 = ten(lambda e: e.matmul(psY0[:, 0:Q], oner, rows[:, 0:Q],
                                       start=True, stop=True),
                    waits=[t_f2, t_psy_evac[0]])
        t_rbc = sca(lambda e: e.copy(mtb[:, 0:Q], psY0[:, 0:Q]),
                    waits=[t_fb2] + list(mtb_free[0]))
        t_psSM_free = [t_f1]
        th_g1 = vec(lambda e: e.tensor_copy(
            winb[0][:, 0:2048], ytl[:, 2048:4096]),
            waits=[free_winb[0], max(t_y_parts[8:16])])
        t_ysp1 = vec(lambda e: e.tensor_tensor(
            xsp[:, 0:2048], ytl[:, 2048:4096], winb[0][:, 0:2048],
            ALU.subtract), waits=[th_g1, free_xsp])
        if r + 1 < n_rounds:
            emit_silus_for(r + 1)
            emit_btok(r + 1)
        t_warm_h = sca(lambda e: e.activation(scr[:, Q:Q + 1], scr[:, Q:Q + 1],
                                              AF.Exp),
                       waits=[scr_free[1]])
        scr_free[1] = t_warm_h

        if STOP_PHASE == "F":
            break
        seq_wh = [(h, w * 4 + mi) for h in range(2) for w in range(2)
                  for mi in range(4)]
        t_wo = {}
        t_wo[(seq_wh[0][1], seq_wh[0][0])] = wout_dma(
            seq_wh[0][1] % 2, seq_wh[0][1], seq_wh[0][0], [free_woutb[seq_wh[0][1] % 2]])
        t_wo[(seq_wh[1][1], seq_wh[1][0])] = wout_dma(
            seq_wh[1][1] % 2, seq_wh[1][1], seq_wh[1][0], [free_woutb[seq_wh[1][1] % 2]])
        g_evac = [None] * FC
        wv_last = None
        g_h0_last = None
        si = 0
        for half in range(2):
            t_ysp = t_ysp0 if half == 0 else t_ysp1
            rhs_hi = (lambda kc8: winb[1][:, kc8 * Q:(kc8 + 1) * Q]) if half == 0 \
                else (lambda kc8: winb[0][:, kc8 * Q:(kc8 + 1) * Q])
            rhs_lo = (lambda kc8: winb[2][:, kc8 * Q:(kc8 + 1) * Q]) \
                if half == 0 \
                else (lambda kc8: xsp[:, kc8 * Q:(kc8 + 1) * Q])
            for mb in range(FC):
                bank = mb % 4
                buf = mb % 2
                for kc8 in range(8):
                    for ph, (wo, rhs) in enumerate(
                            ((0, rhs_hi), (0, rhs_lo), (1024, rhs_hi))):
                        def mmf(e, mb=mb, kc8=kc8, bank=bank, buf=buf,
                                wo=wo, rhs=rhs,
                                st=(kc8 == 0 and ph == 0),
                                sp=(kc8 == 7 and ph == 2)):
                            return e.matmul(
                                psBIG[:, bank * 512:bank * 512 + Q],
                                woutb[buf][:, wo + kc8 * 128:wo + (kc8 + 1) * 128],
                                rhs(kc8),
                                start=st, stop=sp)
                        wts = []
                        if kc8 == 0 and ph == 0:
                            wts = [t_wo[(mb, half)], t_ysp, bigfree[bank]]
                        last = ten(mmf, waits=wts)
                if si + 2 < len(seq_wh):
                    nh, nmb = seq_wh[si + 2]
                    if (nmb, nh) not in t_wo:
                        t_wo[(nmb, nh)] = wout_dma(nmb % 2, nmb, nh, [last])
                si += 1
                wv_last = last
                par = mb % 2
                tg1 = vec(lambda e, bank=bank, par=par: e.tensor_tensor(
                    scr[:, par * Q:(par + 1) * Q],
                    psBIG[:, bank * 512:bank * 512 + Q], mtb[:, 0:Q], ALU.mult),
                    waits=[last, free_x0, t_rbc, scr_free[par]])
                tev = vec(lambda e, mb=mb, par=par, x0=x0: e.tensor_tensor(
                    x0[:, mb * Q:(mb + 1) * Q], x0[:, mb * Q:(mb + 1) * Q],
                    scr[:, par * Q:(par + 1) * Q], ALU.add),
                    waits=[tg1])
                scr_free[par] = tev
                bigfree[bank] = tev
                g_evac[mb] = tev
            if half == 0:
                g_h0_last = wv_last
        free_ytl = wv_last
        free_xsp = wv_last
        free_winb[0] = wv_last
        free_winb[1] = g_h0_last
        free_winb[2] = g_h0_last
        free_woutb[0] = free_woutb[1] = wv_last
        mtb_free[0] = [scr_free[0]]

        tb_last_ln = None

        def ln_phase(src_get, apply_out_get, eps_ap, inv_ap, t_src, extra_apply_w=(),
                     pair_t=None):
            nonlocal t_psSM_free
            sum_last = None
            for fci in range(FC):
                def smf(e, src=src_get(fci), st=(fci == 0), sp=(fci == FC - 1)):
                    return e.matmul(psSM[0:1, 0:Q], onec[:], src,
                                    start=st, stop=sp)
                wts = [t_src[fci]]
                if fci == 0:
                    wts += t_psSM_free
                sum_last = ten(smf, waits=wts)
            sq_last = None
            for fci in range(FC):
                par = fci % 2
                tsq = sca(lambda e, src=src_get(fci), par=par: e.activation(
                    scr[:, par * Q:(par + 1) * Q], src, AF.Square),
                    waits=[t_src[fci], scr_free[par]])
                def smf2(e, par=par, st=(fci == 0), sp=(fci == FC - 1)):
                    return e.matmul(psSM[0:1, Q:2 * Q], onec[:],
                                    scr[:, par * Q:(par + 1) * Q], start=st, stop=sp)
                sq_last = ten(smf2, waits=[tsq] + ([sum_last] if fci == 0 else []))
                scr_free[par] = sq_last
            t_m = vec(lambda e: e.tensor_scalar(
                rows[:, 2 * Q:3 * Q], psSM[0:1, 0:Q], inv_ap, None, ALU.mult),
                waits=[sq_last])
            t_m2 = vec(lambda e: e.tensor_tensor(
                rows[:, 0:Q], rows[:, 2 * Q:3 * Q], rows[:, 2 * Q:3 * Q],
                ALU.mult), waits=[t_m])
            t_var = vec(lambda e: e.scalar_tensor_tensor(
                rows[:, Q:2 * Q], psSM[0:1, 0 + Q:2 * Q], inv_ap,
                rows[:, 0:Q], ALU.mult, ALU.subtract),
                waits=[t_m2])
            t_l1 = sca(lambda e: e.activation(rows[:, Q:2 * Q], rows[:, Q:2 * Q],
                                              AF.Ln, bias=eps_ap), waits=[t_var])
            t_l2 = sca(lambda e: e.activation(rows[:, 0:Q], rows[:, Q:2 * Q],
                                              AF.Exp, scale=c_nhalf),
                       waits=[t_l1, t_var])
            t_mr = vec(lambda e: e.tensor_tensor(rows[:, Q:2 * Q],
                                                 rows[:, 2 * Q:3 * Q],
                                                 rows[:, 0:Q], ALU.mult),
                       waits=[t_l2, t_l1])
            tb1 = ten(lambda e: e.matmul(psY0[:, 0:Q], oner, rows[:, 0:Q],
                                         start=True, stop=True),
                      waits=[t_l2, t_psy_free_ln[0]])
            tb2 = ten(lambda e: e.matmul(psY1[:, 0:Q], oner,
                                         rows[:, Q:2 * Q],
                                         start=True, stop=True),
                      waits=[t_mr, t_psy_free_ln[1]])
            ap_toks = []
            src_t, s_str, dst_t, d_str = pair_t
            for i2 in range(FC // 2):
                so = bass.AP(src_t, i2 * 2 * Q, [[s_str, 128], [Q, 2], [1, Q]])
                do = bass.AP(dst_t, i2 * 2 * Q, [[d_str, 128], [Q, 2], [1, Q]])
                y0b = bass.AP(psY0, 0, [[512, 128], [0, 2], [1, Q]])
                y1b = bass.AP(psY1, 0, [[512, 128], [0, 2], [1, Q]])
                v1 = vec(lambda e, do=do, so=so, y0b=y0b: e.tensor_tensor(
                    do, so, y0b, ALU.mult),
                    waits=[tb1, P.tok("SV"), P.tok("SS")] + list(extra_apply_w))
                v2 = vec(lambda e, do=do, y1b=y1b: e.tensor_tensor(
                    do, do, y1b, ALU.subtract), waits=[tb2, v1])
                ap_toks.append(v2)
                ap_toks.append(v2)
            nonlocal tb_last_ln
            tb_last_ln = tb2
            t_psSM_free = [t_l1, t_var, t_m]
            return ap_toks, [ap_toks[-1]]

        t_psy_free_ln = [t_rbc, t_psy_evac[1]]
        ln1_ap, psy_readers = ln_phase(
            lambda fci: x0[:, fci * Q:(fci + 1) * Q],
            lambda fci: x0[:, fci * Q:(fci + 1) * Q],
            c_eps5, c_i1024, g_evac, pair_t=(x0, FC * Q, x0, FC * Q))
        t_psy_free_ln = [ln1_ap[-1], ln1_ap[-1]]

        if STOP_PHASE == "H":
            break
        cast_toks = []
        for fci in range(FC):
            sl4 = fci % 4
            tc1 = sca(lambda e, fci=fci, sl4=sl4, x0=x0: e.copy(
                xb16[:, sl4 * Q:(sl4 + 1) * Q], x0[:, fci * Q:(fci + 1) * Q]),
                waits=[ln1_ap[fci], t_xb16_free[sl4]])
            tc2 = vec(lambda e, fci=fci, sl4=sl4: e.tensor_scalar(
                xq8[:, fci * Q:(fci + 1) * Q], xb16[:, sl4 * Q:(sl4 + 1) * Q],
                sccol.ap()[:, 0:1], None, ALU.mult), waits=[tc1])
            cast_toks.append(tc2)
            t_xb16_free[sl4] = tc2
        ffn_evac = [None] * FC
        t_t1_free = None
        t1_hist = []
        tv2_hist = []
        for mb in range(FC):
            bank = mb % 4
            last = None
            for kc in range(FC):
                def mmf(e, mb=mb, kc=kc, bank=bank, st=(kc == 0), sp=(kc == FC - 1)):
                    return e.matmul(psBIG[:, bank * 512:bank * 512 + Q],
                                    wf8[:, kc * HID + mb * 128:kc * HID + (mb + 1) * 128],
                                    xq8[:, kc * Q:(kc + 1) * Q], start=st, stop=sp)
                wts = [cast_toks[kc]]
                if kc == 0:
                    wts += [bigfree[bank], t_wf8[0], t_wf8[1]]
                last = ten(mmf, waits=wts)
            tb = xsp[:, mb * Q:(mb + 1) * Q]
            tv1 = sca(lambda e, mb=mb, bank=bank, tb=tb: e.activation(
                tb, psBIG[:, bank * 512:bank * 512 + Q],
                AF.Identity, scale=sccol.ap()[:, 1:2],
                bias=fbc32.ap()[:, mb:mb + 1]),
                waits=[last, t_fb32, free_xsp if mb == 0 else None])
            tv2 = tv1
            bigfree[bank] = tv1
            tv2_hist.append(tv2)
            if mb % 2 == 1:
                i2 = mb // 2
                tv3 = vec(lambda e, i2=i2, x0=x0: e.tensor_tensor(
                    x0[:, i2 * 2 * Q:(i2 + 1) * 2 * Q],
                    x0[:, i2 * 2 * Q:(i2 + 1) * 2 * Q],
                    xsp[:, i2 * 2 * Q:(i2 + 1) * 2 * Q], ALU.add),
                    waits=[tv2_hist[mb - 1], tv2, ln1_ap[mb - 1], ln1_ap[mb],
                           cast_toks[mb - 1], cast_toks[mb]])
                t1_hist.append(tv3)
                ffn_evac[mb - 1] = tv3
                ffn_evac[mb] = tv3

        if STOP_PHASE == "I":
            break
        free_xsp = t1_hist[-1] if t1_hist else free_xsp
        if r + 1 < n_rounds:
            D6_pre[r + 1] = emit_d6block(r + 1)
        ln2_ap, _ = ln_phase(
            lambda fci: x0[:, fci * Q:(fci + 1) * Q],
            lambda fci: ytl[:, fci * Q:(fci + 1) * Q],
            c_eps5, c_i1024, ffn_evac, pair_t=(x0, FC * Q, ytl, PC * Q))

        t_psy_free = [ln2_ap[-1], ln2_ap[-1]]

        mtb_free[1] = [tb_last_ln, ln2_ap[-1]]
        t_out = out_dma(r, waits=ln2_ap + [free_ytl])
        free_xbc = t_out
        free_ytl = t_out
        x0_next_waits = [ln2_ap[-1], P.tok("SS")]

    P.op("sync", None, waits=[P.tok("SO")])
    P.op("gpsimd", None, waits=[P.tok("SO")])

    block = es.enter_context(nc.Block())
    for name in ("sync", "gpsimd", "tensor", "vector", "scalar"):
        def make(nm):
            def body(eng):
                P.emit(nm, eng, sems)
            return body
        getattr(block, name)(make(name))
    es.close()
    return nc


_host_cache = {}


def _prep_host(inputs):
    xi = np.ascontiguousarray(np.asarray(inputs["L0_attn_hidden_states"], np.float32))
    w_in_t = np.asarray(inputs["L0_attn_in_proj_w"], np.float32).T
    w_in = np.zeros((NBLK_IN, 128, FC * 128), np.float32)
    for blk in range(NBLK_IN):
        bs = 32 if blk == NBLK_IN - 1 else 128
        blkw = w_in_t[:, 128 * blk:128 * blk + bs]
        w_in[blk, :, 0:FC * bs] = blkw.reshape(FC, 128, bs).transpose(1, 0, 2).reshape(128, FC * bs)
    w_out_t = np.asarray(inputs["L0_attn_out_proj_w"], np.float32).T
    w_out = np.zeros((FC, 128, PC * 128), np.float32)
    for blk in range(FC):
        blkw = w_out_t[:, 128 * blk:128 * blk + 128]
        w_out[blk] = blkw.reshape(PC, 128, 128).transpose(1, 0, 2).reshape(128, PC * 128)
    w_ffn = np.ascontiguousarray(np.asarray(inputs["L0_ffn_kernel"]))
    cw = np.asarray(inputs["L0_attn_conv_w"], np.float32)
    convw = np.ascontiguousarray(cw.reshape(CC, 128, KCONV).transpose(1, 0, 2).reshape(128, CC * KCONV))
    cb = np.asarray(inputs["L0_attn_conv_b"], np.float32)
    convb = np.ascontiguousarray(cb.reshape(CC, 128).T)
    dtb = np.ascontiguousarray(np.asarray(inputs["L0_attn_dt_bias"], np.float32).reshape(NH, 1))
    alog = np.ascontiguousarray(np.asarray(inputs["L0_attn_A_log"], np.float32).reshape(NH, 1))
    dvec = np.ascontiguousarray(
        np.repeat(np.asarray(inputs["L0_attn_D"], np.float32), PH).reshape(PC, 128).T)
    nwv = np.ascontiguousarray(
        np.asarray(inputs["L0_attn_norm_weight"], np.float32).reshape(PC, 128).T)
    fbias = np.ascontiguousarray(
        np.asarray(inputs["L0_ffn_bias"]).reshape(FC, 128).T)
    sin = np.asarray(inputs["L0_ffn_input_scale"], np.float32).reshape(1, 1)
    skk = np.asarray(inputs["L0_ffn_kernel_scale"], np.float32).reshape(1, 1)
    idn = np.eye(128, dtype=np.float32)
    i_idx = np.arange(Q)
    mka = np.zeros((128, 2 * Q), np.float32)
    for jb in range(2):
        j_glob = jb * 128 + np.arange(128)
        mka[:, jb * Q:(jb + 1) * Q] = np.where(
            i_idx[None, :] >= j_glob[:, None], 0.0, NEG_BIG).astype(np.float32)
    common = dict(win=w_in, wout=w_out, wffn=w_ffn, convw=convw, convb=convb,
                  dtb=dtb, alog=alog, dvec=dvec, nwv=nwv, fbias=fbias,
                  sin=sin, skk=skk, idn=idn, mka=mka)
    return xi, common


def _shard(xi, ci, per=NSEQ):
    a = xi[ci * per:(ci + 1) * per].reshape(per, NCHUNK, Q, FC, 128)
    return np.ascontiguousarray(a.transpose(0, 1, 4, 3, 2).reshape(
        per * NCHUNK, 128, FC * Q))


def _unshard(arr, per=NSEQ):
    a = np.asarray(arr, np.float32).reshape(per, NCHUNK, 128, FC, Q)
    return np.ascontiguousarray(a.transpose(0, 1, 4, 3, 2)).reshape(
        per, SEQLEN, HID)


def kernel(**inputs):
    xi, common = _prep_host(inputs)
    B = xi.shape[0]
    n_cores = 8
    per = B // n_cores
    nc = build_program()
    in_maps = []
    for ci in range(n_cores):
        m = dict(common)
        m["xin"] = _shard(xi, ci, per)
        in_maps.append(m)
    res = run_bass_kernel_spmd(nc, in_maps, core_ids=list(range(n_cores)))
    out = np.concatenate(
        [_unshard(r["yout"], per) for r in res.results], axis=0)
    return out
